# version 01
# baseline (speedup 1.0000x reference)
import contextlib

import numpy as np

import concourse.bass as bass
import concourse.mybir as mybir
from concourse.bass_utils import run_bass_kernel_spmd

F32 = mybir.dt.float32
BF16 = mybir.dt.bfloat16
I32 = mybir.dt.int32
AF = mybir.ActivationFunctionType
ALU = mybir.AluOpType
AX = mybir.AxisListType

T = 2048
D = 2048
ND = D // 128
NCH = T // 512
EPS = 1e-5
ENGINES = ("sync", "tensor", "vector", "scalar", "gpsimd")
FULLSYNC = False

SIGMA = np.concatenate([np.arange(0, D, 2), np.arange(1, D, 2)])
SIGMA_INV = np.argsort(SIGMA)


class Sem:
    __slots__ = ("h", "n")

    def __init__(self, h):
        self.h = h
        self.n = 0

    def inc(self, k=1):
        self.n += k
        return self.n


class KB:

    def __init__(self, nc, es):
        self.nc = nc
        self.es = es
        self.ops = {e: [] for e in ENGINES}
        self._last_wait = {}
        self.input_names = []
        self.bank_state = [None] * 8

    def sem(self, name):
        return Sem(self.es.enter_context(self.nc.semaphore(name)))

    def sbuf(self, name, shape, dt=F32):
        return self.es.enter_context(self.nc.sbuf_tensor(name, shape, dt))

    def psum(self, name, shape, dt=F32):
        return self.es.enter_context(self.nc.psum_tensor(name, shape, dt))

    def inp(self, name, shape, dt=F32):
        self.input_names.append(name)
        return self.nc.dram_tensor(name, shape, dt, kind="ExternalInput").ap()

    def outp(self, name, shape, dt=F32):
        return self.nc.dram_tensor(name, shape, dt, kind="ExternalOutput").ap()

    def scratch(self, name, shape, dt):
        return self.nc.dram_tensor(name, shape, dt, kind="Internal").ap()

    def wait(self, eng, sem, tgt=None):
        if tgt is None:
            tgt = sem.n
        key = (eng, id(sem))
        if self._last_wait.get(key, -1) >= tgt:
            return
        self._last_wait[key] = tgt
        self.ops[eng].append(lambda e, s=sem.h, v=tgt: e.wait_ge(s, v))

    def E(self, eng, meth, *args, sem=None, inc=1, **kw):
        if sem is not None:
            sem.inc(inc)

        def f(e, meth=meth, args=args, kw=kw, sem=sem, inc=inc):
            ins = getattr(e, meth)(*args, **kw)
            if sem is not None:
                ins.then_inc(sem.h, inc)
        self.ops[eng].append(f)
        return sem.n if sem is not None else None

    def dma(self, q, dst, src, sem, **kw):
        return self.E(q, "dma_start", dst, src, sem=sem, inc=16, **kw)

    def vfl(self, sem):
        self.E("vector", "tensor_copy", self.flt[0:1, 4:8], self.flt[0:1, 0:4],
               sem=sem)
        return sem.n

    def sfl(self, sem):
        self.E("scalar", "copy", self.flt[0:1, 12:16], self.flt[0:1, 8:12],
               sem=sem)
        return sem.n

    def bank_wait(self, b):
        st = self.bank_state[b]
        if st is not None:
            self.wait("tensor", st[0], st[1])

    def bank_set(self, b, sem, tgt):
        self.bank_state[b] = (sem, tgt)

    def finalize(self):
        block = self.es.enter_context(self.nc.Block())
        ops = self.ops

        @block.sync
        def _(e):
            for f in ops["sync"]:
                f(e)

        @block.tensor
        def _(e):
            for f in ops["tensor"]:
                f(e)

        @block.vector
        def _(e):
            for f in ops["vector"]:
                f(e)

        @block.scalar
        def _(e):
            for f in ops["scalar"]:
                f(e)

        @block.gpsimd
        def _(e):
            for f in ops["gpsimd"]:
                f(e)


FC_ONE, FC_EPS, FC_HALF, FC_M75, FC_P75 = 0, 1, 2, 3, 4


def fc(c, idx, rows=128):
    return c["s_fcol"][0:rows, idx:idx + 1]


def alloc_common(kb):
    c = {"kb": kb}
    c["xln"] = kb.sbuf("xln", [128, ND * T], BF16)
    c["PW"] = [kb.sbuf(f"pw{i}", [128, 8192], BF16) for i in range(4)]
    c["PF"] = [kb.sbuf(f"pf{i}", [128, 2048], F32) for i in range(4)]
    c["PO"] = [kb.sbuf(f"po{i}", [128, 2048], BF16) for i in range(2)]
    c["strow"] = kb.sbuf("strow", [128, T], F32)
    c["mbc"] = kb.sbuf("mbc", [128, 512], F32)
    c["rbc"] = kb.sbuf("rbc", [128, 512], F32)
    c["ones128"] = kb.inp("c_ones128", [128, 1], BF16)
    c["ones1f"] = kb.inp("c_ones1f", [128, 128], F32)
    c["fcol"] = kb.inp("c_fcol", [128, 8], F32)
    c["s_ones128"] = kb.sbuf("s_ones128", [128, 1], BF16)
    c["s_ones1f"] = kb.sbuf("s_ones1f", [128, 128], F32)
    c["s_fcol"] = kb.sbuf("s_fcol", [128, 8], F32)
    csem = kb.sem("consts")
    kb.dma("sync", c["s_ones128"][:, :], c["ones128"][:, :], csem)
    kb.dma("sync", c["s_ones1f"][:, :], c["ones1f"][:, :], csem)
    kb.dma("sync", c["s_fcol"][:, :], c["fcol"][:, :], csem)
    c["csem"] = csem
    c["PB"] = [kb.psum(f"pb{i}", [128, 512], F32) for i in range(8)]
    kb.flt = kb.sbuf("flt", [1, 16], F32)
    kb.E("vector", "memzero", kb.flt[0:1, 0:8])
    kb.E("scalar", "memzero", kb.flt[0:1, 8:16])
    return c


def emit_stage_x(kb, c, src_dram, sfx):
    dsem = [kb.sem(f"stgx{i}" + sfx) for i in range(4)]
    ssem = kb.sem("stgxs" + sfx)
    buf = c["PF"][2]
    for dt in range(ND):
        for ch in range(NCH):
            k = dt * NCH + ch
            s = k % 4
            slot = buf[:, s * 512:(s + 1) * 512]
            if k >= 4:
                kb.wait("sync", ssem, k - 3)
            kb.dma("sync", slot,
                   src_dram[dt * 128:(dt + 1) * 128,
                            ch * 512:(ch + 1) * 512], dsem[s])
            kb.wait("scalar", dsem[s], dsem[s].n)
            kb.E("scalar", "copy",
                 c["xln"][:, dt * T + ch * 512: dt * T + (ch + 1) * 512],
                 slot, sem=ssem)
    kb.sfl(ssem)
    return ssem, ssem.n


def emit_ln(kb, c, sfx, ready_sem, ready_tgt):
    PB = c["PB"]
    xln = c["xln"]
    st = c["strow"]
    lnt = kb.sem("lnt" + sfx)
    lnv = kb.sem("lnv" + sfx)
    lns = kb.sem("lns" + sfx)

    for b in range(8):
        kb.bank_wait(b)
    kb.wait("tensor", ready_sem, ready_tgt)
    kb.wait("tensor", c["csem"], c["csem"].n)
    kb.wait("scalar", c["csem"], c["csem"].n)
    kb.wait("vector", c["csem"], c["csem"].n)

    for dt in range(ND):
        sqb = c["PO"][dt % 2]
        if dt >= 2:
            kb.wait("scalar", lnt, dt - 1)
        kb.wait("scalar", ready_sem, ready_tgt)
        kb.E("scalar", "activation", sqb[:, :], xln[:, dt * T:(dt + 1) * T],
             AF.Square)
        kb.sfl(lns)
        kb.wait("tensor", lns, lns.n)
        for ch in range(NCH):
            kb.E("tensor", "matmul", PB[ch][0:1, :],
                 c["s_ones128"][:, :],
                 xln[:, dt * T + ch * 512: dt * T + (ch + 1) * 512],
                 start=(dt == 0), stop=(dt == ND - 1))
            kb.E("tensor", "matmul", PB[4 + ch][0:1, :],
                 c["s_ones128"][:, :], sqb[:, ch * 512:(ch + 1) * 512],
                 start=(dt == 0), stop=(dt == ND - 1),
                 sem=lnt if ch == NCH - 1 else None)

    xtmp = c["PF"][3][:, 0:512]
    mrow = st[0:1, 0:512]
    vrow = st[0:1, 512:1024]
    trow = st[0:1, 1024:1536]
    kb.wait("vector", lnt, ND)
    for ch in range(NCH):
        kb.E("vector", "tensor_scalar_mul", mrow, PB[ch][0:1, :], 1.0 / D)
        kb.E("vector", "tensor_scalar_mul", vrow, PB[4 + ch][0:1, :], 1.0 / D)
        kb.E("vector", "tensor_mul", trow, mrow, mrow)
        kb.E("vector", "tensor_sub", vrow, vrow, trow)
        kb.vfl(lnv)
        kb.wait("scalar", lnv, lnv.n)
        kb.E("scalar", "activation", trow, vrow, AF.Ln, bias=fc(c, FC_EPS, 1))
        kb.E("scalar", "activation", vrow, trow, AF.Exp, scale=-0.5)
        kb.sfl(lns)
        kb.wait("tensor", lnv, lnv.n)
        kb.wait("tensor", lns, lns.n)
        kb.E("tensor", "matmul", PB[ch][:, :], c["s_ones1f"][0:1, :],
             mrow, start=True, stop=True)
        kb.E("tensor", "matmul", PB[4 + ch][:, :], c["s_ones1f"][0:1, :],
             vrow, start=True, stop=True, sem=lnt)
        kb.wait("vector", lnt, lnt.n)
        for _ in range(8):
            kb.vfl(lnv)
        kb.E("vector", "tensor_copy", c["mbc"][:, :], PB[ch][:, :])
        kb.E("vector", "tensor_copy", c["rbc"][:, :], PB[4 + ch][:, :],
             sem=lnv)
        kb.bank_set(ch, lnv, lnv.n)
        kb.bank_set(4 + ch, lnv, lnv.n)
        for dt in range(ND):
            a0 = dt * T + ch * 512
            kb.E("vector", "tensor_sub", xtmp, xln[:, a0:a0 + 512],
                 c["mbc"][:, :])
            kb.E("vector", "tensor_mul", xln[:, a0:a0 + 512], xtmp,
                 c["rbc"][:, :], sem=lnv)
    kb.vfl(lnv)
    return lnv, lnv.n



def emit_l1ffn(kb, c, sfx, wgate_b, wup_b, wdown_b, inter_dram,
               xsrc, xdst, wready, ready, scale=0.5):
    PB = c["PB"]
    xln = c["xln"]
    PW, PF, PO = c["PW"], c["PF"], c["PO"]
    NJ = 16

    wsem = [kb.sem("l1f_w0" + sfx), kb.sem("l1f_w1" + sfx)]
    msem = kb.sem("l1f_m" + sfx)
    vsem = kb.sem("l1f_v" + sfx)
    ssem = kb.sem("l1f_s" + sfx)
    isem = kb.sem("l1f_i" + sfx)
    isl = [kb.sem("l1f_ia" + sfx), kb.sem("l1f_ib" + sfx)]
    desm = [kb.sem("l1f_d0" + sfx), kb.sem("l1f_d1" + sfx)]
    xsem = kb.sem("l1f_x" + sfx)
    osem = kb.sem("l1f_o" + sfx)
    epv = kb.sem("l1f_e" + sfx)
    mmB = kb.sem("l1f_mb" + sfx)

    wg_sb = [PW[0], PW[1]]
    wu_sb = [PW[2], PW[3]]
    sg, it, ab, tp = PF[0], PF[1], PF[2], PF[3]
    ob = [PO[0], PO[1]]

    for (s, t) in wready:
        kb.wait("sync", s, t)
    for eng in ("tensor", "scalar", "vector"):
        kb.wait(eng, ready[0], ready[1])

    for j in range(NJ):
        bsel = j % 2
        if j >= 2:
            kb.wait("sync", msem, 32 * (j - 1))
        kb.dma("sync",
               wg_sb[bsel][:, :].rearrange("p (h w) -> p h w", h=16),
               wgate_b[:, j * 512:(j + 1) * 512].rearrange(
                   "(h p) w -> p h w", p=128), wsem[bsel])
        kb.dma("sync",
               wu_sb[bsel][:, :].rearrange("p (h w) -> p h w", h=16),
               wup_b[:, j * 512:(j + 1) * 512].rearrange(
                   "(h p) w -> p h w", p=128), wsem[bsel])
        kb.wait("tensor", wsem[bsel], wsem[bsel].n)
        for ch in range(NCH):
            for r in range(4):
                kb.bank_wait(2 * r)
                kb.bank_wait(2 * r + 1)
                wsb, usb = wg_sb[bsel], wu_sb[bsel]
                for ht in range(16):
                    lhs_g = wsb[:, ht * 512 + r: ht * 512 + 512: 4]
                    lhs_u = usb[:, ht * 512 + r: ht * 512 + 512: 4]
                    rhs = xln[:, ht * T + ch * 512: ht * T + (ch + 1) * 512]
                    kb.E("tensor", "matmul", PB[2 * r][:, :], lhs_g, rhs,
                         start=(ht == 0), stop=(ht == 15),
                         sem=msem if ht == 15 else None)
                    kb.E("tensor", "matmul", PB[2 * r + 1][:, :], lhs_u, rhs,
                         start=(ht == 0), stop=(ht == 15),
                         sem=msem if ht == 15 else None)
            kb.wait("scalar", vsem, max(0, vsem.n - 4))
            for r in range(4):
                kb.wait("scalar", msem, msem.n)
                kb.E("scalar", "activation", sg[:, r * 512:(r + 1) * 512],
                     PB[2 * r][:, :], AF.Silu, sem=ssem)
            kb.sfl(ssem)
            for r in range(4):
                kb.wait("vector", ssem, ssem.n)
                kb.wait("vector", msem, msem.n)
                kb.E("vector", "tensor_mul", it[:, r * 512:(r + 1) * 512],
                     PB[2 * r + 1][:, :], sg[:, r * 512:(r + 1) * 512],
                     sem=vsem)
            for r in range(4):
                kb.bank_set(2 * r, ssem, ssem.n)
                kb.bank_set(2 * r + 1, vsem, vsem.n - 3 + r)
            for r in range(4):
                kb.E("vector", "scalar_tensor_tensor",
                     ab[:, r * 512:(r + 1) * 512],
                     it[:, r * 512:(r + 1) * 512], -1.0,
                     it[:, r * 512:(r + 1) * 512], ALU.mult, ALU.max)
            A = [ab[:, r * 512:(r + 1) * 512] for r in range(4)]
            Tp = [tp[:, r * 512:(r + 1) * 512] for r in range(4)]
            kb.E("vector", "tensor_max", Tp[0], A[0], A[1])
            kb.E("vector", "tensor_tensor", Tp[1], A[0], A[1], ALU.min)
            kb.E("vector", "tensor_max", Tp[2], A[2], A[3])
            kb.E("vector", "tensor_tensor", Tp[3], A[2], A[3], ALU.min)
            kb.E("vector", "tensor_tensor", Tp[0], Tp[0], Tp[2], ALU.min)
            kb.E("vector", "tensor_max", Tp[1], Tp[1], Tp[3])
            kb.E("vector", "tensor_max", Tp[0], Tp[0], Tp[1])
            k_glob = j * NCH + ch
            obuf = ob[k_glob % 2]
            if k_glob >= 2:
                kb.wait("vector", isl[k_glob % 2], isl[k_glob % 2].n)
            for r in range(4):
                kb.E("vector", "tensor_tensor", Tp[1], A[r], Tp[0],
                     ALU.is_ge)
                kb.E("vector", "tensor_mul", obuf[:, r * 512:(r + 1) * 512],
                     it[:, r * 512:(r + 1) * 512], Tp[1], sem=vsem)
            kb.vfl(vsem)
            kb.wait("sync", vsem, vsem.n)
            kb.dma("sync",
                   inter_dram[j * 512:(j + 1) * 512,
                              ch * 512:(ch + 1) * 512].rearrange(
                       "(r p) w -> p r w", p=128),
                   obuf[:, :].rearrange("p (r w) -> p r w", r=4),
                   isl[k_glob % 2])
            isem.inc(0)

    wd_sb = [PW[0], PW[1]]
    itb = [PW[2], PW[3]]

    def xo_slot(ht):
        return (PF[0] if ht < 4 else PF[1])[:, (ht % 4) * 512:
                                            (ht % 4 + 1) * 512]

    def xn_slot(ht):
        return (PF[2] if ht < 4 else PF[3])[:, (ht % 4) * 512:
                                            (ht % 4 + 1) * 512]

    kb.wait("sync", isl[0], isl[0].n)
    kb.wait("sync", isl[1], isl[1].n)
    for ch in range(NCH):
        for hs in range(2):
            k_iter = ch * 2 + hs
            if k_iter >= 1:
                kb.wait("sync", epv, 8 * k_iter)
            for ht in range(8):
                dtile = hs * 8 + ht
                kb.dma("sync", xo_slot(ht),
                       xsrc[dtile * 128:(dtile + 1) * 128,
                            ch * 512:(ch + 1) * 512], xsem)
            for b in range(8):
                kb.bank_wait(b)
            for j in range(NJ):
                bsel = j % 2
                if j >= 2:
                    kb.wait("sync", mmB, mmB.n - 4)
                kb.dma("sync",
                       wd_sb[bsel][:, 0:4096].rearrange(
                           "p (r w) -> p r w", r=4),
                       wdown_b[j * 512:(j + 1) * 512,
                               hs * 1024:(hs + 1) * 1024].rearrange(
                           "(p r) w -> p r w", r=4), desm[bsel])
                kb.dma("sync",
                       itb[bsel][:, 0:2048].rearrange("p (r w) -> p r w", r=4),
                       inter_dram[j * 512:(j + 1) * 512,
                                  ch * 512:(ch + 1) * 512].rearrange(
                           "(r p) w -> p r w", p=128), desm[bsel])
                kb.wait("tensor", desm[bsel], desm[bsel].n)
                for r in range(4):
                    for ht in range(8):
                        kb.E("tensor", "matmul", PB[ht][:, :],
                             wd_sb[bsel][:, r * 1024 + ht * 128:
                                         r * 1024 + (ht + 1) * 128],
                             itb[bsel][:, r * 512:(r + 1) * 512],
                             start=(j == 0 and r == 0),
                             stop=(j == NJ - 1 and r == 3),
                             sem=mmB if ht == 7 else None)
            kb.wait("vector", mmB, mmB.n)
            kb.wait("vector", xsem, xsem.n)
            kb.wait("vector", osem, osem.n)
            kb.wait("vector", ssem, ssem.n)
            for ht in range(8):
                kb.E("vector", "scalar_tensor_tensor", xn_slot(ht),
                     PB[ht][:, :], scale, xo_slot(ht), ALU.mult, ALU.add,
                     sem=epv)
                kb.bank_set(ht, epv, epv.n)
            kb.vfl(epv)
            kb.wait("sync", epv, epv.n)
            kb.wait("scalar", epv, epv.n)
            for ht in range(8):
                dtile = hs * 8 + ht
                kb.dma("sync", xdst[dtile * 128:(dtile + 1) * 128,
                                    ch * 512:(ch + 1) * 512],
                       xn_slot(ht), osem)
                kb.E("scalar", "copy",
                     xln[:, dtile * T + ch * 512: dtile * T + (ch + 1) * 512],
                     xn_slot(ht), sem=ssem)
    kb.sfl(ssem)
    return (ssem, ssem.n), (osem, osem.n)


class Prep:

    def __init__(self, kb):
        self.kb = kb
        self.pf = [kb.sbuf(f"prep_f{i}", [128, 1024], F32) for i in range(2)]
        self.po = [kb.sbuf(f"prep_o{i}", [128, 1024], BF16) for i in range(2)]
        self.ld = [kb.sem("prep_ld0"), kb.sem("prep_ld1")]
        self.cv = kb.sem("prep_cv")
        self.st = [kb.sem("prep_st0"), kb.sem("prep_st1")]
        self.k = 0

    def cast(self, name, src_ap, shape):
        kb = self.kb
        dst = kb.scratch(name, shape, BF16)
        rows, cols = shape
        for r0 in range(0, rows, 128):
            rr = min(128, rows - r0)
            for c0 in range(0, cols, 1024):
                cc = min(1024, cols - c0)
                s = self.k % 2
                if self.k >= 2:
                    kb.wait("sync", self.cv, self.k - 1)
                kb.dma("sync", self.pf[s][0:rr, 0:cc],
                       src_ap[r0:r0 + rr, c0:c0 + cc], self.ld[s])
                kb.wait("vector", self.ld[s], self.ld[s].n)
                if self.k >= 2:
                    kb.wait("vector", self.st[s], self.st[s].n)
                kb.E("vector", "tensor_copy", self.po[s][0:rr, 0:cc],
                     self.pf[s][0:rr, 0:cc])
                kb.vfl(self.cv)
                kb.wait("sync", self.cv, self.cv.n)
                kb.dma("sync", dst[r0:r0 + rr, c0:c0 + cc],
                       self.po[s][0:rr, 0:cc], self.st[s])
                self.k += 1
        return dst

    def done(self):
        ret = []
        for s in (0, 1):
            if self.st[s].n:
                ret.append((self.st[s], self.st[s].n))
        return ret



def host_consts():
    import ml_dtypes
    ones128 = np.ones((128, 1), dtype=ml_dtypes.bfloat16)
    ones1f = np.ones((128, 128), dtype=np.float32)
    fcol = np.zeros((128, 8), dtype=np.float32)
    fcol[:, FC_ONE] = 1.0
    fcol[:, FC_EPS] = EPS
    fcol[:, FC_HALF] = 0.5
    fcol[:, FC_M75] = -7.5
    fcol[:, FC_P75] = 7.5
    return dict(c_ones128=ones128, c_ones1f=ones1f, c_fcol=fcol)


def kernel(**inputs):
    raise NotImplementedError("assembled incrementally; see side tests")



def emit_outproj(kb, c, sfx, w_b, act_dram, xsrc, xdst, scale, bias_pp,
                 aready):
    PB = c["PB"]
    xln = c["xln"]
    PW, PF = c["PW"], c["PF"]
    wot = [PW[0], PW[1]]
    yft = PW[3]
    wosem = [kb.sem("op_w0" + sfx), kb.sem("op_w1" + sfx)]
    yftsem = kb.sem("op_yf" + sfx)
    xsem = kb.sem("op_x" + sfx)
    osem = kb.sem("op_o" + sfx)
    epv = kb.sem("op_e" + sfx)
    ssem = kb.sem("op_s" + sfx)
    mmB = kb.sem("op_m" + sfx)

    def xo_slot(ht):
        return (PF[0] if ht < 4 else PF[1])[:, (ht % 4) * 512:
                                            (ht % 4 + 1) * 512]

    def xn_slot(ht):
        return (PF[2] if ht < 4 else PF[3])[:, (ht % 4) * 512:
                                            (ht % 4 + 1) * 512]

    for (s, t) in aready:
        kb.wait("sync", s, t)
    for ch in range(NCH):
        if ch >= 1:
            kb.wait("sync", mmB, mmB.n)
        kb.dma("sync", yft[:, :].rearrange("p (kt w) -> p kt w", kt=ND),
               act_dram[:, ch * 512:(ch + 1) * 512].rearrange(
                   "(kt p) w -> p kt w", p=128), yftsem)
        kb.wait("tensor", yftsem, yftsem.n)
        for hs in range(2):
            k_iter = ch * 2 + hs
            if k_iter >= 1:
                kb.wait("sync", epv, epv.n)
            for ht in range(8):
                dtile = hs * 8 + ht
                kb.dma("sync", xo_slot(ht),
                       xsrc[dtile * 128:(dtile + 1) * 128,
                            ch * 512:(ch + 1) * 512], xsem)
            for b in range(8):
                kb.bank_wait(b)
            for kt in range(ND):
                bsel = kt % 2
                if kt >= 2:
                    kb.wait("sync", mmB, mmB.n - 1)
                kb.dma("sync", wot[bsel][:, 0:1024],
                       w_b[kt * 128:(kt + 1) * 128,
                           hs * 1024:(hs + 1) * 1024], wosem[bsel])
                kb.wait("tensor", wosem[bsel], wosem[bsel].n)
                for ht in range(8):
                    kb.E("tensor", "matmul", PB[ht][:, :],
                         wot[bsel][:, ht * 128:(ht + 1) * 128],
                         yft[:, kt * 512:(kt + 1) * 512],
                         start=(kt == 0), stop=(kt == ND - 1),
                         sem=mmB if ht == 7 else None)
            kb.wait("vector", mmB, mmB.n)
            kb.wait("vector", xsem, xsem.n)
            kb.wait("vector", osem, osem.n)
            kb.wait("vector", ssem, ssem.n)
            for ht in range(8):
                dtile = hs * 8 + ht
                if bias_pp is None:
                    kb.E("vector", "scalar_tensor_tensor", xn_slot(ht),
                         PB[ht][:, :], scale, xo_slot(ht), ALU.mult,
                         ALU.add, sem=epv)
                else:
                    kb.E("vector", "scalar_tensor_tensor", xn_slot(ht),
                         PB[ht][:, :], bias_pp[:, dtile:dtile + 1],
                         xo_slot(ht), ALU.add, ALU.add, sem=epv)
                kb.bank_set(ht, epv, epv.n)
            kb.vfl(epv)
            kb.wait("sync", epv, epv.n)
            kb.wait("scalar", epv, epv.n)
            for ht in range(8):
                dtile = hs * 8 + ht
                kb.dma("sync", xdst[dtile * 128:(dtile + 1) * 128,
                                    ch * 512:(ch + 1) * 512],
                       xn_slot(ht), osem)
                kb.E("scalar", "copy",
                     xln[:, dtile * T + ch * 512: dtile * T + (ch + 1) * 512],
                     xn_slot(ht), sem=ssem)
    kb.sfl(ssem)
    return (ssem, ssem.n), (osem, osem.n)



def emit_l1attn(kb, c, sfx, wx_b, wg1_b, wo1_b, bgpp_d, at_d, dppp_d,
                onehot_d, id128_d, ybsel_d, gate_dram, y_dram, yf_dram,
                xsrc, xdst, wready, ready):
    PB = c["PB"]
    xln = c["xln"]
    PW, PF, PO = c["PW"], c["PF"], c["PO"]

    csem = kb.sem("l1a_c" + sfx)
    wsem = [kb.sem("l1a_w0" + sfx), kb.sem("l1a_w1" + sfx)]
    msem = kb.sem("l1a_m" + sfx)
    vsem = kb.sem("l1a_v" + sfx)
    ssem = kb.sem("l1a_s" + sfx)
    gsem = kb.sem("l1a_g" + sfx)
    ysem = kb.sem("l1a_y" + sfx)
    yfsm = kb.sem("l1a_yf" + sfx)
    desm = kb.sem("l1a_d" + sfx)
    xsem = kb.sem("l1a_x" + sfx)
    osem = kb.sem("l1a_o" + sfx)
    epv = kb.sem("l1a_e" + sfx)
    mmB = kb.sem("l1a_mb" + sfx)

    wx_sb = PW[3][:, 0:16 * 288]
    oh_sb = PW[0][0:32, 0:32 * 128]
    id_sb = PW[0][:, 4096:4224]
    ytmp = PW[0][0:1, 0:4 * T]
    yrow = PW[1][0:1, 0:T]
    at_sb = kb.sbuf("l1a_at" + sfx, [128, 32], F32)
    dppp_sb = kb.sbuf("l1a_dp" + sfx, [128, 16], F32)
    bgpp_sb = kb.sbuf("l1a_bg" + sfx, [128, 16], F32)
    yb_sb = kb.sbuf("l1a_yb" + sfx, [1, 4 * 128], BF16)
    db_sb = kb.sbuf("l1a_db" + sfx, [32, T], BF16)

    for (s, t) in wready:
        kb.wait("sync", s, t)
    for eng in ("tensor", "scalar", "vector"):
        kb.wait(eng, ready[0], ready[1])
    kb.dma("sync", wx_sb.rearrange("p (dt f) -> p dt f", dt=ND),
           wx_b[:, :].rearrange("(dt p) f -> p dt f", p=128), csem)
    kb.dma("sync", at_sb[:, :], at_d[:, :], csem)
    kb.dma("sync", dppp_sb[:, :], dppp_d[:, :], csem)
    kb.dma("sync", bgpp_sb[:, :], bgpp_d[:, :], csem)
    kb.dma("sync", yb_sb[:, :], ybsel_d[:, :], csem)
    kb.wait("tensor", csem, csem.n)

    B_T, C_T = PO[0], PO[1]
    for fi, dst in ((0, B_T), (1, C_T)):
        for ch in range(NCH):
            b = ch % 2
            kb.bank_wait(b)
            for dt in range(ND):
                kb.E("tensor", "matmul", PB[b][:, :],
                     wx_sb[:, dt * 288 + fi * 128: dt * 288 + (fi + 1) * 128],
                     xln[:, dt * T + ch * 512: dt * T + (ch + 1) * 512],
                     start=(dt == 0), stop=(dt == ND - 1),
                     sem=msem if dt == ND - 1 else None)
            kb.wait("scalar", msem, msem.n)
            kb.E("scalar", "copy", dst[:, ch * 512:(ch + 1) * 512],
                 PB[b][:, :], sem=ssem)
            kb.bank_set(b, ssem, ssem.n)
    for ch in range(NCH):
        b = 2 + ch % 2
        kb.bank_wait(b)
        for dt in range(ND):
            kb.E("tensor", "matmul", PB[b][0:32, :],
                 wx_sb[:, dt * 288 + 256: dt * 288 + 288],
                 xln[:, dt * T + ch * 512: dt * T + (ch + 1) * 512],
                 start=(dt == 0), stop=(dt == ND - 1),
                 sem=msem if dt == ND - 1 else None)
        kb.wait("scalar", msem, msem.n)
        kb.E("scalar", "copy", db_sb[:, ch * 512:(ch + 1) * 512],
             PB[b][0:32, :], sem=ssem)
        kb.bank_set(b, ssem, ssem.n)

    wgt = [PW[0], PW[1]]
    gstage = PW[2]
    for dmt in range(ND):
        b = dmt % 2
        if dmt >= 2:
            kb.wait("sync", msem, msem.n - 4)
        kb.dma("sync", wgt[b][:, 0:2048].rearrange("p (dt w) -> p dt w", dt=ND),
               wg1_b[:, dmt * 128:(dmt + 1) * 128].rearrange(
                   "(dt p) w -> p dt w", p=128), wsem[b])
        kb.wait("tensor", wsem[b], wsem[b].n)
        for ch in range(NCH):
            pb = 4 + ch
            kb.bank_wait(pb)
            for dt in range(ND):
                kb.E("tensor", "matmul", PB[pb][:, :],
                     wgt[b][:, dt * 128:(dt + 1) * 128],
                     xln[:, dt * T + ch * 512: dt * T + (ch + 1) * 512],
                     start=(dt == 0), stop=(dt == ND - 1),
                     sem=msem if dt == ND - 1 else None)
        kb.wait("scalar", msem, msem.n)
        if dmt >= 2:
            kb.wait("scalar", gsem, gsem.n)
        for ch in range(NCH):
            pb = 4 + ch
            kb.E("scalar", "activation",
                 gstage[:, (ch if dmt % 2 == 0 else 4 + ch) * 512:
                        (ch if dmt % 2 == 0 else 4 + ch) * 512 + 512],
                 PB[pb][:, :], AF.Sigmoid, bias=bgpp_sb[:, dmt:dmt + 1],
                 sem=ssem)
            kb.bank_set(pb, ssem, ssem.n)
        kb.sfl(ssem)
        kb.wait("sync", ssem, ssem.n)
        off = 0 if dmt % 2 == 0 else 4 * 512
        kb.dma("sync", gate_dram[dmt * 128:(dmt + 1) * 128, :],
               gstage[:, off:off + 2048], gsem)

    kb.wait("sync", msem, msem.n)
    csem2 = kb.sem("l1a_c2" + sfx)
    kb.dma("sync", oh_sb, onehot_d[:, :], csem2)
    kb.dma("sync", id_sb, id128_d[:, :], csem2)
    kb.wait("tensor", csem2, csem2.n)
    a_buf, u_buf, hs_buf, tmpf = PF[0], PF[1], PF[2], PF[3]
    P_bf = PW[3]
    for h in range(32):
        for ch in range(NCH):
            kb.bank_wait(ch)
            kb.E("tensor", "matmul", PB[ch][:, :], id_sb,
                 B_T[:, ch * 512:(ch + 1) * 512], start=True, stop=False)
            kb.E("tensor", "matmul", PB[ch][:, :],
                 oh_sb[:, h * 128:(h + 1) * 128],
                 db_sb[:, ch * 512:(ch + 1) * 512], start=False, stop=True,
                 sem=msem)
        kb.wait("scalar", msem, msem.n)
        kb.wait("scalar", vsem, vsem.n)
        for ch in range(NCH):
            s0 = ch * 512
            kb.E("scalar", "activation", tmpf[:, s0:s0 + 512], PB[ch][:, :],
                 AF.Exp)
            kb.E("scalar", "activation", tmpf[:, s0:s0 + 512],
                 tmpf[:, s0:s0 + 512], AF.Ln, bias=fc(c, FC_ONE))
            kb.E("scalar", "activation", u_buf[:, s0:s0 + 512],
                 tmpf[:, s0:s0 + 512], AF.Square, sem=ssem)
            kb.bank_set(ch, ssem, ssem.n)
        kb.sfl(ssem)
        kb.wait("vector", ssem, ssem.n)
        kb.E("vector", "tensor_scalar_mul", a_buf[:, :], tmpf[:, :],
             at_sb[:, h:h + 1], sem=vsem)
        kb.vfl(vsem)
        kb.wait("scalar", vsem, vsem.n)
        kb.E("scalar", "activation", a_buf[:, :], a_buf[:, :], AF.Exp)
        kb.sfl(ssem)
        kb.wait("vector", ssem, ssem.n)
        kb.E("vector", "tensor_tensor_scan", hs_buf[:, :], a_buf[:, :],
             u_buf[:, :], 0.0, ALU.mult, ALU.add)
        kb.E("vector", "tensor_mul", P_bf[:, 0:T], hs_buf[:, :], C_T[:, :])
        kb.vfl(vsem)
        kb.wait("tensor", vsem, vsem.n)
        kb.wait("tensor", c["csem"], c["csem"].n)
        for ch in range(NCH):
            pb = 4 + ch
            kb.bank_wait(pb)
            kb.E("tensor", "matmul", PB[pb][0:1, :], c["s_ones128"][:, :],
                 P_bf[:, ch * 512:(ch + 1) * 512], start=True, stop=True,
                 sem=msem if ch == NCH - 1 else None)
        kb.wait("scalar", msem, msem.n)
        if h >= 1:
            kb.wait("scalar", ysem, ysem.n)
        for ch in range(NCH):
            pb = 4 + ch
            kb.E("scalar", "copy", yrow[0:1, ch * 512:(ch + 1) * 512],
                 PB[pb][0:1, :], sem=ssem)
            kb.bank_set(pb, ssem, ssem.n)
        kb.sfl(ssem)
        kb.wait("sync", ssem, ssem.n)
        kb.dma("sync", y_dram[h:h + 1, :], yrow, ysem)

    gtile = PW[2]
    yfst = [PF[0], PF[1]]
    kb.wait("sync", ysem, ysem.n)
    for dmt in range(ND):
        if dmt >= 1:
            kb.wait("sync", msem, msem.n)
            kb.wait("sync", yfsm, yfsm.n)
        kb.dma("sync", ytmp,
               y_dram[4 * (dmt % 8): 4 * (dmt % 8) + 4, :].rearrange(
                   "(o a) t -> o (a t)", o=1), desm)
        kb.dma("sync", gtile[:, (dmt % 2) * 2048:(dmt % 2) * 2048 + 2048],
               gate_dram[dmt * 128:(dmt + 1) * 128, :], desm)
        kb.wait("tensor", desm, desm.n)
        kb.wait("vector", yfsm, yfsm.n)
        for ch in range(NCH):
            pb = ch % 2
            kb.bank_wait(pb)
            for s in range(4):
                kb.E("tensor", "matmul", PB[pb][:, :],
                     yb_sb[0:1, s * 128:(s + 1) * 128],
                     ytmp[0:1, s * T + ch * 512: s * T + (ch + 1) * 512],
                     start=(s == 0), stop=(s == 3),
                     sem=msem if s == 3 else None)
            kb.wait("vector", msem, msem.n)
            yslot = yfst[ch % 2][:, (ch // 2) * 512:(ch // 2) * 512 + 512]
            kb.E("vector", "tensor_scalar_mul", yslot,
                 xln[:, dmt * T + ch * 512: dmt * T + (ch + 1) * 512],
                 dppp_sb[:, dmt:dmt + 1])
            kb.E("vector", "tensor_add", yslot, yslot, PB[pb][:, :],
                 sem=vsem)
            kb.bank_set(pb, vsem, vsem.n)
            kb.E("vector", "tensor_mul",
                 P_bf[:, ch * 512:(ch + 1) * 512], yslot,
                 gtile[:, (dmt % 2) * 2048 + ch * 512:
                       (dmt % 2) * 2048 + (ch + 1) * 512], sem=vsem)
        kb.vfl(vsem)
        kb.wait("sync", vsem, vsem.n)
        kb.dma("sync", yf_dram[dmt * 128:(dmt + 1) * 128, :], P_bf[:, 0:T],
               yfsm)

    st, ot = emit_outproj(kb, c, sfx + "op", wo1_b, yf_dram, xsrc, xdst,
                          1.0, None, [(yfsm, yfsm.n)])
    return st, ot


def host_l1attn_inputs(Wx, Wg, bg, Wo, A, Dp):
    import ml_dtypes
    d = {}
    d["l1a_wx"] = np.ascontiguousarray(Wx[SIGMA, :])
    d["l1a_wg"] = np.ascontiguousarray(Wg[SIGMA][:, SIGMA])
    d["l1a_wo"] = np.ascontiguousarray(Wo[SIGMA][:, SIGMA])
    d["l1a_bg"] = np.ascontiguousarray(bg[SIGMA].reshape(16, 128).T)
    d["l1a_at"] = np.ascontiguousarray(A.T.astype(np.float32))
    d["l1a_dp"] = np.ascontiguousarray(
        Dp[SIGMA // 64].reshape(16, 128).T.astype(np.float32))
    oh = np.zeros((32, 32 * 128), dtype=ml_dtypes.bfloat16)
    for h in range(32):
        oh[h, h * 128:(h + 1) * 128] = 1.0
    d["l1a_oh"] = oh
    d["l1a_id"] = np.eye(128, dtype=ml_dtypes.bfloat16)
    yb = np.zeros((1, 4 * 128), dtype=ml_dtypes.bfloat16)
    for s in range(4):
        yb[0, s * 128 + s * 32: s * 128 + (s + 1) * 32] = 1.0
    d["l1a_yb"] = yb
    return d


def dummy_mm(kb, c, scrap_bank, msem):
    kb.bank_wait(scrap_bank)
    kb.E("tensor", "matmul", c["PB"][scrap_bank][0:1, :],
         c["s_ones128"][0:1, 0:1], c["xln"][0:1, 0:512],
         start=True, stop=True, sem=msem)
    kb.bank_set(scrap_bank, msem, msem.n)
    return msem.n


def emit_l0attn(kb, c, sfx, wqkv_tb, wout_tb, bqk_d, bvbc_d, boutpp_d,
                alibi_d, cmask_d, ctx_dram, xsrc, xdst, wready, ready):
    PB = c["PB"]
    xln = c["xln"]
    PW, PF, PO = c["PW"], c["PF"], c["PO"]
    SC = 1.0 / float(np.sqrt(128.0))

    csem = kb.sem("l0a_c" + sfx)
    vwsem = kb.sem("l0a_vw" + sfx)
    wqks = [kb.sem("l0a_w0" + sfx), kb.sem("l0a_w1" + sfx)]
    msem = kb.sem("l0a_m" + sfx)
    vsem = kb.sem("l0a_v" + sfx)
    ssem = kb.sem("l0a_s" + sfx)
    csp = [kb.sem("l0a_cs0" + sfx), kb.sem("l0a_cs1" + sfx)]

    bqk_sb = kb.sbuf("l0a_bqk" + sfx, [128, 32], F32)
    bv_sb = kb.sbuf("l0a_bv" + sfx, [128, T], F32)
    bout_sb = kb.sbuf("l0a_bo" + sfx, [128, 16], F32)
    al_sb = kb.sbuf("l0a_al" + sfx, [128, 16 * 34], F32)
    cm_sb = kb.sbuf("l0a_cm" + sfx, [128, 4 * 512], BF16)

    qt_sb = PW[0][:, 0:2048]
    kt_sb = PW[0][:, 2048:4096]
    vn_sb = PW[2]
    p_sb = PW[1]
    wtile = PW[3]
    rrow = PF[0][0:1, 0:512]
    rb_sb = PF[0][:, 512:1024]

    for (s, t) in wready:
        kb.wait("sync", s, t)
    for eng in ("tensor", "scalar", "vector"):
        kb.wait(eng, ready[0], ready[1])
    kb.dma("sync", bqk_sb[:, :], bqk_d[:, :], csem)
    kb.dma("sync", bv_sb[:, :], bvbc_d[:, :], csem)
    kb.dma("sync", bout_sb[:, :], boutpp_d[:, :], csem)
    kb.dma("sync", al_sb[:, :], alibi_d[:, :], csem)
    kb.dma("sync", cm_sb[:, :], cmask_d[:, :], csem)
    kb.wait("tensor", csem, csem.n)
    kb.wait("scalar", csem, csem.n)
    kb.wait("vector", csem, csem.n)

    for g in range(4):
        kb.wait("sync", msem, msem.n)
        kb.dma("sync", wtile[:, :].rearrange("p (dt w) -> p dt w", dt=ND),
               wqkv_tb[:, 4096 + g * 512: 4096 + (g + 1) * 512].rearrange(
                   "(dt p) w -> p dt w", p=128), vwsem)
        kb.wait("tensor", vwsem, vwsem.n)
        vg_tgts = []

        def v_evac(tt):
            b = tt % 4
            kb.wait("vector", msem, vg_tgts[tt] + 1)
            kb.E("vector", "scalar_tensor_tensor",
                 vn_sb[:, tt * 512:(tt + 1) * 512], PB[b][:, :], 1.0,
                 bv_sb[:, g * 512:(g + 1) * 512], ALU.mult, ALU.add,
                 sem=vsem)
            kb.bank_set(b, vsem, vsem.n)

        for tt in range(16):
            b = tt % 4
            kb.bank_wait(b)
            for dt in range(ND):
                kb.E("tensor", "matmul", PB[b][:, :],
                     xln[:, dt * T + tt * 128: dt * T + (tt + 1) * 128],
                     wtile[:, dt * 512:(dt + 1) * 512],
                     start=(dt == 0), stop=(dt == ND - 1),
                     sem=msem if dt == ND - 1 else None)
            vg_tgts.append(msem.n)
            if tt >= 1:
                v_evac(tt - 1)
        dummy_mm(kb, c, 6, msem)
        v_evac(15)
        kb.vfl(vsem)

        for hh in range(4):
            h = g * 4 + hh
            par = h % 2
            kb.wait("sync", msem, msem.n)
            kb.dma("sync",
                   wtile[:, par * 4096: par * 4096 + 2048].rearrange(
                       "p (dt w) -> p dt w", dt=ND),
                   wqkv_tb[:, h * 128:(h + 1) * 128].rearrange(
                       "(dt p) w -> p dt w", p=128), wqks[par])
            kb.dma("sync",
                   wtile[:, par * 4096 + 2048: par * 4096 + 4096].rearrange(
                       "p (dt w) -> p dt w", dt=ND),
                   wqkv_tb[:, 2048 + h * 128: 2048 + (h + 1) * 128].rearrange(
                       "(dt p) w -> p dt w", p=128), wqks[par])
            kb.wait("tensor", wqks[par], wqks[par].n)
            qk_tgts = []

            def qk_evac(i):
                which, ch, b, tgt = qk_tgts[i]
                kb.wait("scalar", msem, tgt + 1)
                dst = (qt_sb if which == 0 else kt_sb)
                kb.E("scalar", "activation", dst[:, ch * 512:(ch + 1) * 512],
                     PB[b][:, :], AF.Identity,
                     bias=bqk_sb[:, 16 * which + h: 16 * which + h + 1],
                     sem=ssem)
                kb.bank_set(b, ssem, ssem.n)

            seq = [(0, 0, ch) for ch in range(NCH)] +                   [(1, 2048, ch) for ch in range(NCH)]
            for i, (which, base, ch) in enumerate(seq):
                b = 4 + 2 * which + ch % 2
                kb.bank_wait(b)
                for dt in range(ND):
                    kb.E("tensor", "matmul", PB[b][:, :],
                         wtile[:, par * 4096 + base + dt * 128:
                               par * 4096 + base + (dt + 1) * 128],
                         xln[:, dt * T + ch * 512: dt * T + (ch + 1) * 512],
                         start=(dt == 0), stop=(dt == ND - 1),
                         sem=msem if dt == ND - 1 else None)
                qk_tgts.append((which, ch, b, msem.n))
                if i >= 1:
                    qk_evac(i - 1)
            dummy_mm(kb, c, 0, msem)
            qk_evac(7)
            kb.sfl(ssem)
            bh = 512 if h >= 4 else (256 if h >= 2 else 128)

            def p_exp(k2, qc):
                for qs in range(0, 512, bh):
                    qg = qc * 512 + qs
                    if k2 * 128 >= qg + bh:
                        kb.E("scalar", "memzero",
                             p_sb[:, k2 * 512 + qs: k2 * 512 + qs + bh])
                        continue
                    m64 = (qg + bh // 2 - k2 * 128) // 64 + 2
                    kb.E("scalar", "activation",
                         p_sb[:, k2 * 512 + qs: k2 * 512 + qs + bh],
                         PB[k2 % 4][:, qs:qs + bh], AF.Exp,
                         bias=al_sb[:, h * 34 + m64: h * 34 + m64 + 1],
                         scale=SC)
                kb.E("scalar", "copy", kb.flt[0:1, 12:16],
                     kb.flt[0:1, 8:12], sem=ssem)
                kb.bank_set(k2 % 4, ssem, ssem.n)

            for qc in range(NCH):
                nkt = 4 * qc + 4
                kb.wait("tensor", ssem, ssem.n)
                sc_tgts = []
                for kt in range(nkt):
                    b = kt % 4
                    kb.bank_wait(b)
                    kb.E("tensor", "matmul", PB[b][:, :],
                         kt_sb[:, kt * 128:(kt + 1) * 128],
                         qt_sb[:, qc * 512:(qc + 1) * 512],
                         start=True, stop=True, sem=msem)
                    sc_tgts.append(msem.n)
                    if kt >= 3:
                        k2 = kt - 3
                        kb.wait("scalar", msem, sc_tgts[k2] + 1)
                        p_exp(k2, qc)
                dummy_mm(kb, c, 7, msem)
                for k2 in range(max(0, nkt - 3), nkt):
                    kb.wait("scalar", msem, sc_tgts[k2] + 1)
                    p_exp(k2, qc)
                kb.sfl(ssem)
                kb.wait("vector", ssem, ssem.n)
                for r in range(4):
                    kt = 4 * qc + r
                    kb.E("vector", "tensor_mul",
                         p_sb[:, kt * 512:(kt + 1) * 512],
                         p_sb[:, kt * 512:(kt + 1) * 512],
                         cm_sb[:, r * 512:(r + 1) * 512], sem=vsem)
                kb.vfl(vsem)
                kb.wait("tensor", ssem, ssem.n)
                kb.wait("tensor", vsem, vsem.n)
                kb.bank_wait(4)
                kb.bank_wait(5)
                for kt in range(nkt):
                    kb.E("tensor", "matmul", PB[4][0:1, :],
                         c["s_ones128"][:, :], p_sb[:, kt * 512:(kt + 1) * 512],
                         start=(kt == 0), stop=(kt == nkt - 1),
                         sem=msem if kt == nkt - 1 else None)
                den_tgt = msem.n
                for kt in range(nkt):
                    kb.E("tensor", "matmul", PB[5][:, :],
                         vn_sb[:, kt * 512 + hh * 128: kt * 512 + (hh + 1) * 128],
                         p_sb[:, kt * 512:(kt + 1) * 512],
                         start=(kt == 0), stop=(kt == nkt - 1),
                         sem=msem if kt == nkt - 1 else None)
                ctx_tgt = msem.n
                kb.wait("vector", msem, den_tgt + 1)
                kb.E("vector", "reciprocal", rrow, PB[4][0:1, :])
                kb.vfl(vsem)
                kb.bank_set(4, vsem, vsem.n)
                kb.bank_wait(6)
                kb.wait("tensor", vsem, vsem.n)
                kb.E("tensor", "matmul", PB[6][:, :], c["s_ones1f"][0:1, :],
                     rrow, start=True, stop=True)
                kb.E("tensor", "matmul", PB[6][:, :], c["s_ones1f"][0:1, :],
                     rrow, start=True, stop=True, sem=msem)
                rb_tgt = msem.n
                kb.wait("scalar", msem, rb_tgt)
                kb.E("scalar", "copy", rb_sb, PB[6][:, :], sem=ssem)
                kb.bank_set(6, ssem, ssem.n)
                kb.sfl(ssem)
                slot = (h * NCH + qc) % 2
                kb.wait("vector", ssem, ssem.n)
                kb.wait("vector", msem, ctx_tgt + 1)
                if h * NCH + qc >= 2:
                    kb.wait("vector", csp[slot], csp[slot].n)
                kb.E("vector", "tensor_mul", PO[0][:, slot * 512:(slot + 1) * 512],
                     PB[5][:, :], rb_sb, sem=vsem)
                kb.bank_set(5, vsem, vsem.n)
                kb.vfl(vsem)
                kb.wait("sync", vsem, vsem.n)
                kb.dma("sync", ctx_dram[h * 128:(h + 1) * 128,
                                        qc * 512:(qc + 1) * 512],
                       PO[0][:, slot * 512:(slot + 1) * 512], csp[slot])

    st, ot = emit_outproj(kb, c, sfx + "op", wout_tb, ctx_dram, xsrc, xdst,
                          1.0, bout_sb,
                          [(csp[0], csp[0].n), (csp[1], csp[1].n)])
    return st, ot


def host_l0attn_inputs(Wqkv, b_qkv, Wout, b_out):
    import ml_dtypes
    d = {}
    d["l0a_wqkvt"] = np.ascontiguousarray(Wqkv.T[SIGMA, :])
    d["l0a_woutt"] = np.ascontiguousarray(Wout.T[:, SIGMA])
    d["l0a_bqk"] = np.ascontiguousarray(
        b_qkv[:4096].reshape(32, 128).T.astype(np.float32))
    d["l0a_bv"] = np.ascontiguousarray(np.broadcast_to(
        b_qkv[4096:], (128, 2048)).astype(np.float32))
    d["l0a_bout"] = np.ascontiguousarray(
        b_out[SIGMA].reshape(16, 128).T.astype(np.float32))
    slopes = 1.0 / 2.0 ** ((np.arange(16, dtype=np.float64) + 1) * 0.5)
    al = np.zeros((128, 16 * 34), dtype=np.float32)
    p = np.arange(128, dtype=np.float64)
    for h in range(16):
        for u in range(34):
            m = (u - 2) * 64
            al[:, h * 34 + u] = (slopes[h] * (p - m)).astype(np.float32)
    d["l0a_alibi"] = al
    cm = np.zeros((128, 4 * 512), dtype=ml_dtypes.bfloat16)
    for r in range(4):
        q = np.arange(512)[None, :]
        p = np.arange(128)[:, None]
        cm[:, r * 512:(r + 1) * 512] = (q >= r * 128 + p).astype(
            ml_dtypes.bfloat16)
    d["l0a_cmask"] = cm
    return d


SIGMA_N = np.concatenate([np.arange(0, 16384, 2), np.arange(1, 16384, 2)])


def emit_l0decode(kb, c, sfx):
    PW, PF, PO = c["PW"], c["PF"], c["PO"]
    pk_g = kb.inp("l0f_pkg", [16384, 1024], I32)
    sc_g = kb.inp("l0f_scg", [16384, 64], F32)
    pk_u = kb.inp("l0f_pku", [16384, 1024], I32)
    sc_u = kb.inp("l0f_scu", [16384, 64], F32)
    pk_d = kb.inp("l0f_pkd", [2048, 8192], I32)
    sc_d = kb.inp("l0f_scd", [2048, 512], F32)
    wg_dec = kb.scratch("wg_dec" + sfx, [16384, 2048], BF16)
    wu_dec = kb.scratch("wu_dec" + sfx, [16384, 2048], BF16)
    wd_dec = kb.scratch("wd_dec" + sfx, [2048, 16384], BF16)

    spk = [kb.sbuf(f"dc_pk{i}" + sfx, [128, 1024], I32) for i in range(2)]
    lihi = kb.sbuf("dc_lh" + sfx, [128, 2048], I32)
    c2 = PO[0]
    m2 = PW[0][:, 0:2048]
    tt_ = PW[0][:, 2048:4096]
    l2 = PW[0][:, 4096:6144]
    aa = PW[0][:, 6144:8192]
    l3 = PW[1][:, 0:2048]
    gg = PW[1][:, 2048:4096]
    sgn = PW[1][:, 4096:6144]
    Wt = PW[2]

    ld = [kb.sem("dc_ld0" + sfx), kb.sem("dc_ld1" + sfx)]
    scl = [kb.sem("dc_sc0" + sfx), kb.sem("dc_sc1" + sfx)]
    wst = [kb.sem("dc_st0" + sfx), kb.sem("dc_st1" + sfx)]
    vsem = kb.sem("dc_v" + sfx)
    ssem = kb.sem("dc_s" + sfx)

    kb.wait("vector", c["csem"], c["csem"].n)
    kb.wait("scalar", c["csem"], c["csem"].n)

    its = []
    for r0 in range(0, 16384, 128):
        its.append((pk_g, sc_g, wg_dec, r0, 0, 1024))
    for r0 in range(0, 16384, 128):
        its.append((pk_u, sc_u, wu_dec, r0, 0, 1024))
    for r0 in range(0, 2048, 128):
        for pc in range(0, 8192, 1024):
            its.append((pk_d, sc_d, wd_dec, r0, pc, 8192))

    for k, (pk, sc, dec, r0, pc, FP) in enumerate(its):
        s = k % 2
        if k >= 2:
            kb.wait("sync", vsem, vsem.n - 2)
        kb.dma("sync", spk[s][:, :], pk[r0:r0 + 128, pc:pc + 1024], ld[s])
        kb.dma("sync", PF[0][:, s * 512: s * 512 + 64],
               sc[r0:r0 + 128, pc // 16: pc // 16 + 64], scl[s])
        kb.wait("vector", ld[s], ld[s].n)
        kb.E("vector", "tensor_single_scalar", lihi[:, 0:1024], spk[s][:, :],
             15, ALU.bitwise_and)
        kb.E("vector", "tensor_scalar", lihi[:, 1024:2048], spk[s][:, :],
             4, 15, ALU.logical_shift_right, ALU.bitwise_and)
        kb.E("vector", "tensor_copy", c2[:, :], lihi[:, :])
        kb.vfl(vsem)
        kb.wait("scalar", vsem, vsem.n)
        kb.E("scalar", "activation", sgn, c2[:, :], AF.Sign,
             bias=fc(c, FC_P75), scale=-1.0)
        kb.sfl(ssem)
        kb.E("vector", "tensor_scalar", m2, c2[:, :], 8.0, -8.0,
             ALU.is_ge, ALU.mult)
        kb.E("vector", "tensor_add", tt_, c2[:, :], m2)
        kb.E("vector", "tensor_scalar_add", l2, tt_, -2.0)
        kb.E("vector", "scalar_tensor_tensor", aa, tt_, 0.5, l2,
             ALU.mult, ALU.max)
        kb.E("vector", "tensor_scalar", l3, tt_, 2.0, -8.0,
             ALU.mult, ALU.add)
        kb.E("vector", "tensor_max", gg, aa, l3)
        kb.wait("vector", ssem, ssem.n)
        kb.E("vector", "tensor_mul", gg, gg, sgn)
        kb.wait("vector", scl[s], scl[s].n)
        kb.wait("vector", wst[s], wst[s].n)
        sc_ap = bass.AP(PF[0], s * 512, [[2048, 128], [1, 64], [0, 16]])
        for pl in range(2):
            kb.E("vector", "tensor_tensor",
                 Wt[:, (2 * s + pl) * 1024: (2 * s + pl) * 1024 + 1024].rearrange(
                     "p (a b) -> p a b", a=64),
                 gg[:, pl * 1024:(pl + 1) * 1024].rearrange(
                     "p (a b) -> p a b", a=64),
                 sc_ap, ALU.mult)
        kb.vfl(vsem)
        kb.wait("sync", vsem, vsem.n)
        if FP == 1024:
            kb.dma("sync", dec[r0:r0 + 128, :],
                   Wt[:, 2 * s * 1024: 2 * s * 1024 + 2048], wst[s])
        else:
            kb.dma("sync", dec[r0:r0 + 128, pc:pc + 1024],
                   Wt[:, 2 * s * 1024: 2 * s * 1024 + 1024], wst[s])
            kb.dma("sync", dec[r0:r0 + 128, 8192 + pc: 8192 + pc + 1024],
                   Wt[:, (2 * s + 1) * 1024: (2 * s + 1) * 1024 + 1024],
                   wst[s])
    done = [(wst[0], wst[0].n), (wst[1], wst[1].n)]
    return wg_dec, wu_dec, wd_dec, done


def host_l0ffn_inputs(gW, gS, uW, uS, dW, dS):
    return {
        "l0f_pkg": np.ascontiguousarray(gW[SIGMA_N, :]),
        "l0f_scg": np.ascontiguousarray(gS[SIGMA_N, :]),
        "l0f_pku": np.ascontiguousarray(uW[SIGMA_N, :]),
        "l0f_scu": np.ascontiguousarray(uS[SIGMA_N, :]),
        "l0f_pkd": np.ascontiguousarray(dW[SIGMA, :]),
        "l0f_scd": np.ascontiguousarray(dS[SIGMA, :]),
    }

def emit_l0ffn(kb, c, sfx, wg_dec, wu_dec, wd_dec, inter_dram,
               xsrc, xdst, wready, ready, scale=0.5):
    PB = c["PB"]
    xln = c["xln"]
    PW, PF, PO = c["PW"], c["PF"], c["PO"]
    NS = 32

    wsem = [kb.sem("l0f_w0" + sfx), kb.sem("l0f_w1" + sfx)]
    msem = kb.sem("l0f_m" + sfx)
    vsem = kb.sem("l0f_v" + sfx)
    ssem = kb.sem("l0f_s" + sfx)
    isl = [kb.sem("l0f_i0" + sfx), kb.sem("l0f_i1" + sfx)]
    desm = [kb.sem("l0f_d0" + sfx), kb.sem("l0f_d1" + sfx)]
    xsem = kb.sem("l0f_x" + sfx)
    osem = kb.sem("l0f_o" + sfx)
    epv = kb.sem("l0f_e" + sfx)
    mmB = kb.sem("l0f_mb" + sfx)

    wg_sb = [PW[0], PW[1]]
    wu_sb = [PW[2], PW[3]]
    sg, up, tp = PF[0], PF[1], PF[3]
    ob = [PO[0], PO[1]]

    for (s, t) in wready:
        kb.wait("sync", s, t)
    for eng in ("tensor", "scalar", "vector"):
        kb.wait(eng, ready[0], ready[1])

    for j in range(NS):
        bsel = j % 2
        if j >= 2:
            kb.wait("sync", msem, 32 * (j - 1))
        for kt in range(ND):
            kb.dma("sync", wg_sb[bsel][:, kt * 512:(kt + 1) * 512],
                   wg_dec[j * 512:(j + 1) * 512, kt * 128:(kt + 1) * 128],
                   wsem[bsel], transpose=True)
            kb.dma("sync", wu_sb[bsel][:, kt * 512:(kt + 1) * 512],
                   wu_dec[j * 512:(j + 1) * 512, kt * 128:(kt + 1) * 128],
                   wsem[bsel], transpose=True)
        kb.wait("tensor", wsem[bsel], wsem[bsel].n)
        for ch in range(NCH):
            for r in range(4):
                kb.bank_wait(2 * r)
                kb.bank_wait(2 * r + 1)
                for kt in range(ND):
                    lhs_g = wg_sb[bsel][:, kt * 512 + r * 128:
                                        kt * 512 + (r + 1) * 128]
                    lhs_u = wu_sb[bsel][:, kt * 512 + r * 128:
                                        kt * 512 + (r + 1) * 128]
                    rhs = xln[:, kt * T + ch * 512: kt * T + (ch + 1) * 512]
                    kb.E("tensor", "matmul", PB[2 * r][:, :], lhs_g, rhs,
                         start=(kt == 0), stop=(kt == ND - 1),
                         sem=msem if kt == ND - 1 else None)
                    kb.E("tensor", "matmul", PB[2 * r + 1][:, :], lhs_u, rhs,
                         start=(kt == 0), stop=(kt == ND - 1),
                         sem=msem if kt == ND - 1 else None)
            kb.wait("scalar", vsem, vsem.n)
            for r in range(4):
                kb.wait("scalar", msem, msem.n - 7 + 2 * r)
                kb.E("scalar", "activation", sg[:, r * 512:(r + 1) * 512],
                     PB[2 * r][:, :], AF.Identity, sem=ssem)
                kb.wait("scalar", msem, msem.n - 6 + 2 * r)
                kb.E("scalar", "activation", up[:, r * 512:(r + 1) * 512],
                     PB[2 * r + 1][:, :], AF.Identity, sem=ssem)
                kb.bank_set(2 * r, ssem, ssem.n)
                kb.bank_set(2 * r + 1, ssem, ssem.n)
            kb.sfl(ssem)
            k_glob = j * NCH + ch
            obuf = ob[k_glob % 2]
            if k_glob >= 2:
                kb.wait("vector", isl[k_glob % 2], isl[k_glob % 2].n)
            kb.wait("vector", ssem, ssem.n)
            for r in range(4):
                g0 = sg[:, r * 512:(r + 1) * 512]
                t1 = tp[:, 0:512]
                t2 = tp[:, 512:1024]
                kb.E("vector", "tensor_mul", t1, g0, g0)
                kb.E("vector", "tensor_scalar", t2, t1, 0.035677408136,
                     0.7978845608, ALU.mult, ALU.add)
                kb.E("vector", "tensor_mul", t1, g0, t2)
                kb.vfl(vsem)
                kb.wait("scalar", vsem, vsem.n)
                kb.E("scalar", "activation", t2, t1, AF.Tanh)
                kb.sfl(ssem)
                kb.wait("vector", ssem, ssem.n)
                kb.E("vector", "scalar_tensor_tensor", t1, t2, 1.0, g0,
                     ALU.add, ALU.mult)
                kb.E("vector", "scalar_tensor_tensor",
                     obuf[:, r * 512:(r + 1) * 512], t1, 0.5,
                     up[:, r * 512:(r + 1) * 512], ALU.mult, ALU.mult,
                     sem=vsem)
            kb.vfl(vsem)
            kb.wait("sync", vsem, vsem.n)
            kb.dma("sync",
                   inter_dram[j * 512:(j + 1) * 512,
                              ch * 512:(ch + 1) * 512].rearrange(
                       "(r p) w -> p r w", p=128),
                   obuf[:, :].rearrange("p (r w) -> p r w", r=4),
                   isl[k_glob % 2])

    wd_sb = [PW[0][:, 0:512], PW[0][:, 512:1024]]
    itb = [PW[2][:, 0:1024], PW[2][:, 1024:2048]]

    def xo_slot(q):
        return (PF[0] if q < 4 else PF[1])[:, (q % 4) * 512:
                                           (q % 4 + 1) * 512]

    def xn_slot(q):
        return (PF[2] if q < 4 else PF[3])[:, (q % 4) * 512:
                                           (q % 4 + 1) * 512]

    kb.wait("sync", isl[0], isl[0].n)
    kb.wait("sync", isl[1], isl[1].n)
    for tq in range(2):
        for dq in range(4):
            k_iter = tq * 4 + dq
            if k_iter >= 1:
                kb.wait("sync", epv, epv.n)
            for q in range(8):
                tile, seg = q // 2, q % 2
                kb.dma("sync", xo_slot(q),
                       xsrc[dq * 512 + tile * 128: dq * 512 + (tile + 1) * 128,
                            tq * 1024 + seg * 512: tq * 1024 + (seg + 1) * 512],
                       xsem)
            for b in range(8):
                kb.bank_wait(b)
            for ci in range(128):
                bsel = ci % 2
                if ci >= 2:
                    kb.wait("sync", mmB, mmB.n - 1)
                kb.dma("sync", wd_sb[bsel],
                       wd_dec[dq * 512:(dq + 1) * 512,
                              ci * 128:(ci + 1) * 128],
                       wsem[bsel], transpose=True)
                kb.dma("sync", itb[bsel],
                       inter_dram[ci * 128:(ci + 1) * 128,
                                  tq * 1024:(tq + 1) * 1024], desm[bsel])
                kb.wait("tensor", wsem[bsel], wsem[bsel].n)
                kb.wait("tensor", desm[bsel], desm[bsel].n)
                for q in range(8):
                    tile, seg = q // 2, q % 2
                    kb.E("tensor", "matmul", PB[q][:, :],
                         wd_sb[bsel][:, tile * 128:(tile + 1) * 128],
                         itb[bsel][:, seg * 512:(seg + 1) * 512],
                         start=(ci == 0), stop=(ci == 127),
                         sem=mmB if q == 7 else None)
            kb.wait("vector", mmB, mmB.n)
            kb.wait("vector", xsem, xsem.n)
            kb.wait("vector", osem, osem.n)
            kb.wait("vector", ssem, ssem.n)
            for q in range(8):
                kb.E("vector", "scalar_tensor_tensor", xn_slot(q),
                     PB[q][:, :], scale, xo_slot(q), ALU.mult, ALU.add,
                     sem=epv)
                kb.bank_set(q, epv, epv.n)
            kb.vfl(epv)
            kb.wait("sync", epv, epv.n)
            kb.wait("scalar", epv, epv.n)
            for q in range(8):
                tile, seg = q // 2, q % 2
                dtile = dq * 4 + tile
                tcol = tq * 1024 + seg * 512
                kb.dma("sync", xdst[dtile * 128:(dtile + 1) * 128,
                                    tcol:tcol + 512], xn_slot(q), osem)
                kb.E("scalar", "copy",
                     xln[:, dtile * T + tcol: dtile * T + tcol + 512],
                     xn_slot(q), sem=ssem)
    kb.sfl(ssem)
    return (ssem, ssem.n), (osem, osem.n)


def build_l0ffn_launch():
    nc = bass.Bass("TRN2", target_bir_lowering=False)
    es = contextlib.ExitStack()
    with es:
        kb = KB(nc, es)
        xt = kb.inp("xt", [D, T], F32)
        yt = kb.outp("yt", [D, T], F32)
        c = alloc_common(kb)
        wg_dec, wu_dec, wd_dec, dready = emit_l0decode(kb, c, "0")
        inter = kb.scratch("inter", [16384, T], BF16)
        stg = emit_stage_x(kb, c, xt, "0")
        lnr = emit_ln(kb, c, "0", *stg)
        (_, _), (ost, ost_n) = emit_l0ffn(
            kb, c, "0", wg_dec, wu_dec, wd_dec, inter, xt, yt, dready, lnr)
        kb.wait("sync", ost, ost_n)
        kb.finalize()
    return nc, kb.input_names


def build_l0attn_launch():
    nc = bass.Bass("TRN2", target_bir_lowering=False)
    es = contextlib.ExitStack()
    with es:
        kb = KB(nc, es)
        xt = kb.inp("xt", [D, T], F32)
        wqkvt = kb.inp("l0a_wqkvt", [D, 6144], F32)
        woutt = kb.inp("l0a_woutt", [D, D], F32)
        bqk = kb.inp("l0a_bqk", [128, 32], F32)
        bv = kb.inp("l0a_bv", [128, T], F32)
        bout = kb.inp("l0a_bout", [128, 16], F32)
        alibi = kb.inp("l0a_alibi", [128, 16 * 34], F32)
        cmask = kb.inp("l0a_cmask", [128, 4 * 512], BF16)
        yt = kb.outp("yt", [D, T], F32)
        c = alloc_common(kb)
        prep = Prep(kb)
        wqkv_b = prep.cast("wqkvt_b", wqkvt, [D, 6144])
        wout_b = prep.cast("woutt_b", woutt, [D, D])
        wcs = prep.done()
        ctx_dram = kb.scratch("ctx_dram", [D, T], BF16)
        stg = emit_stage_x(kb, c, xt, "0")
        lnr = emit_ln(kb, c, "0", *stg)
        (_, _), (ost, ost_n) = emit_l0attn(
            kb, c, "0", wqkv_b, wout_b, bqk, bv, bout, alibi, cmask,
            ctx_dram, xt, yt, wcs, lnr)
        kb.wait("sync", ost, ost_n)
        kb.finalize()
    return nc, kb.input_names


def build_l1ffn_launch():
    nc = bass.Bass("TRN2", target_bir_lowering=False)
    es = contextlib.ExitStack()
    with es:
        kb = KB(nc, es)
        xt = kb.inp("xt", [D, T], F32)
        wgate = kb.inp("l1_wgate", [D, 8192], F32)
        wup = kb.inp("l1_wup", [D, 8192], F32)
        wdown = kb.inp("l1_wdown", [8192, D], F32)
        yt = kb.outp("yt", [D, T], F32)
        c = alloc_common(kb)
        prep = Prep(kb)
        wg_b = prep.cast("wgate_b", wgate, [D, 8192])
        wu_b = prep.cast("wup_b", wup, [D, 8192])
        wd_b = prep.cast("wdown_b", wdown, [8192, D])
        wcs = prep.done()
        inter = kb.scratch("inter", [8192, T], BF16)
        stg = emit_stage_x(kb, c, xt, "0")
        lnr = emit_ln(kb, c, "0", *stg)
        (_, _), (ost, ost_n) = emit_l1ffn(
            kb, c, "0", wg_b, wu_b, wd_b, inter, xt, yt, wcs, lnr)
        kb.wait("sync", ost, ost_n)
        kb.finalize()
    return nc, kb.input_names


def build_l1attn_launch():
    nc = bass.Bass("TRN2", target_bir_lowering=False)
    es = contextlib.ExitStack()
    with es:
        kb = KB(nc, es)
        xt = kb.inp("xt", [D, T], F32)
        wx = kb.inp("l1a_wx", [D, 288], F32)
        wg = kb.inp("l1a_wg", [D, D], F32)
        wo = kb.inp("l1a_wo", [D, D], F32)
        bgpp = kb.inp("l1a_bg", [128, 16], F32)
        at = kb.inp("l1a_at", [128, 32], F32)
        dppp = kb.inp("l1a_dp", [128, 16], F32)
        oh = kb.inp("l1a_oh", [32, 32 * 128], BF16)
        idm = kb.inp("l1a_id", [128, 128], BF16)
        ybs = kb.inp("l1a_yb", [1, 4 * 128], BF16)
        yt = kb.outp("yt", [D, T], F32)
        c = alloc_common(kb)
        prep = Prep(kb)
        wx_b = prep.cast("wx_b", wx, [D, 288])
        wg_b = prep.cast("wg1_b", wg, [D, D])
        wo_b = prep.cast("wo1_b", wo, [D, D])
        wcs = prep.done()
        gate_dram = kb.scratch("gate_dram", [D, T], BF16)
        y_dram = kb.scratch("y_dram", [32, T], BF16)
        yf_dram = kb.scratch("yf_dram", [D, T], BF16)
        stg = emit_stage_x(kb, c, xt, "0")
        lnr = emit_ln(kb, c, "0", *stg)
        (_, _), (ost, ost_n) = emit_l1attn(
            kb, c, "0", wx_b, wg_b, wo_b, bgpp, at, dppp, oh, idm, ybs,
            gate_dram, y_dram, yf_dram, xt, yt, wcs, lnr)
        kb.wait("sync", ost, ost_n)
        kb.finalize()
    return nc, kb.input_names


_PROGRAMS = {}


def _get_program(name):
    if name not in _PROGRAMS:
        _PROGRAMS[name] = {
            "l0ffn": build_l0ffn_launch,
            "l0attn": build_l0attn_launch,
            "l1ffn": build_l1ffn_launch,
            "l1attn": build_l1attn_launch,
        }[name]()
    return _PROGRAMS[name]


def kernel(**inputs):
    inp = {k: np.asarray(v) for k, v in inputs.items()}
    x = inp["L0_attn_x"]
    NC = 8

    consts = host_consts()
    w_l0f = host_l0ffn_inputs(
        inp["L0_ffn_gate_W_packed"], inp["L0_ffn_gate_scales"],
        inp["L0_ffn_up_W_packed"], inp["L0_ffn_up_scales"],
        inp["L0_ffn_down_W_packed"], inp["L0_ffn_down_scales"])
    w_l0a = host_l0attn_inputs(
        inp["L0_attn_Wqkv"], inp["L0_attn_b_qkv"],
        inp["L0_attn_Wout"], inp["L0_attn_b_out"])
    w_l1f = {
        "l1_wgate": np.ascontiguousarray(inp["L1_ffn_w_gate"][SIGMA, :]),
        "l1_wup": np.ascontiguousarray(inp["L1_ffn_w_up"][SIGMA, :]),
        "l1_wdown": np.ascontiguousarray(inp["L1_ffn_w_down"][:, SIGMA]),
    }
    w_l1a = host_l1attn_inputs(
        inp["L1_attn_Wx"], inp["L1_attn_Wg"], inp["L1_attn_bg"],
        inp["L1_attn_Wo"], inp["L1_attn_A"], inp["L1_attn_Dp"])

    cur = [np.ascontiguousarray(x[i].T[SIGMA, :]).astype(np.float32)
           for i in range(NC)]

    seq = [("l0ffn", w_l0f), ("l0attn", w_l0a), ("l0ffn", w_l0f),
           ("l1ffn", w_l1f), ("l1attn", w_l1a), ("l1ffn", w_l1f)]
    for name, wd in seq:
        nc, names = _get_program(name)
        in_maps = []
        for i in range(NC):
            m = dict(consts)
            m.update(wd)
            m["xt"] = cur[i]
            in_maps.append({k: m[k] for k in names})
        res = run_bass_kernel_spmd(nc, in_maps, core_ids=list(range(NC)))
        cur = [np.ascontiguousarray(res.results[i]["yt"]) for i in range(NC)]

    out = np.stack([cur[i][SIGMA_INV, :].T for i in range(NC)], axis=0)
    return out.astype(np.float32)


# version 02
# speedup vs baseline: 1.2303x; 1.2303x over previous
import contextlib

import numpy as np

import concourse.bass as bass
import concourse.mybir as mybir
from concourse.bass_utils import run_bass_kernel_spmd

F32 = mybir.dt.float32
BF16 = mybir.dt.bfloat16
I32 = mybir.dt.int32
AF = mybir.ActivationFunctionType
ALU = mybir.AluOpType
AX = mybir.AxisListType

T = 2048
D = 2048
ND = D // 128
NCH = T // 512
EPS = 1e-5
ENGINES = ("sync", "tensor", "vector", "scalar", "gpsimd")
FULLSYNC = False

SIGMA = np.concatenate([np.arange(0, D, 2), np.arange(1, D, 2)])
SIGMA_INV = np.argsort(SIGMA)


class Sem:
    __slots__ = ("h", "n")

    def __init__(self, h):
        self.h = h
        self.n = 0

    def inc(self, k=1):
        self.n += k
        return self.n


class KB:

    def __init__(self, nc, es):
        self.nc = nc
        self.es = es
        self.ops = {e: [] for e in ENGINES}
        self._last_wait = {}
        self._sems = {}
        self.input_names = []
        self.bank_state = [None] * 8

    def sem(self, name):
        if name not in self._sems:
            self._sems[name] = Sem(
                self.es.enter_context(self.nc.semaphore(name)))
        return self._sems[name]

    def sbuf(self, name, shape, dt=F32):
        return self.es.enter_context(self.nc.sbuf_tensor(name, shape, dt))

    def psum(self, name, shape, dt=F32):
        return self.es.enter_context(self.nc.psum_tensor(name, shape, dt))

    def inp(self, name, shape, dt=F32):
        self.input_names.append(name)
        return self.nc.dram_tensor(name, shape, dt, kind="ExternalInput").ap()

    def outp(self, name, shape, dt=F32):
        return self.nc.dram_tensor(name, shape, dt, kind="ExternalOutput").ap()

    def scratch(self, name, shape, dt):
        return self.nc.dram_tensor(name, shape, dt, kind="Internal").ap()

    def wait(self, eng, sem, tgt=None):
        if tgt is None:
            tgt = sem.n
        key = (eng, id(sem))
        if self._last_wait.get(key, -1) >= tgt:
            return
        self._last_wait[key] = tgt
        self.ops[eng].append(lambda e, s=sem.h, v=tgt: e.wait_ge(s, v))

    def E(self, eng, meth, *args, sem=None, inc=1, **kw):
        if sem is not None:
            sem.inc(inc)

        def f(e, meth=meth, args=args, kw=kw, sem=sem, inc=inc):
            ins = getattr(e, meth)(*args, **kw)
            if sem is not None:
                ins.then_inc(sem.h, inc)
        self.ops[eng].append(f)
        return sem.n if sem is not None else None

    def dma(self, q, dst, src, sem, **kw):
        return self.E(q, "dma_start", dst, src, sem=sem, inc=16, **kw)

    def vfl(self, sem):
        self.E("vector", "tensor_copy", self.flt[0:1, 4:8], self.flt[0:1, 0:4],
               sem=sem)
        return sem.n

    def sfl(self, sem):
        self.E("scalar", "copy", self.flt[0:1, 12:16], self.flt[0:1, 8:12],
               sem=sem)
        return sem.n

    def bank_wait(self, b):
        st = self.bank_state[b]
        if st is not None:
            self.wait("tensor", st[0], st[1])

    def bank_set(self, b, sem, tgt):
        self.bank_state[b] = (sem, tgt)

    def finalize(self):
        block = self.es.enter_context(self.nc.Block())
        ops = self.ops

        @block.sync
        def _(e):
            for f in ops["sync"]:
                f(e)

        @block.tensor
        def _(e):
            for f in ops["tensor"]:
                f(e)

        @block.vector
        def _(e):
            for f in ops["vector"]:
                f(e)

        @block.scalar
        def _(e):
            for f in ops["scalar"]:
                f(e)

        @block.gpsimd
        def _(e):
            for f in ops["gpsimd"]:
                f(e)


FC_ONE, FC_EPS, FC_HALF, FC_M75, FC_P75 = 0, 1, 2, 3, 4


def fc(c, idx, rows=128):
    return c["s_fcol"][0:rows, idx:idx + 1]


def alloc_common(kb):
    c = {"kb": kb}
    c["xln"] = kb.sbuf("xln", [128, ND * T], BF16)
    c["PW"] = [kb.sbuf(f"pw{i}", [128, 8192], BF16) for i in range(4)]
    c["PF"] = [kb.sbuf(f"pf{i}", [128, 2048], F32) for i in range(4)]
    c["PO"] = [kb.sbuf(f"po{i}", [128, 2048], BF16) for i in range(2)]
    c["strow"] = kb.sbuf("strow", [128, T], F32)
    c["mbc"] = kb.sbuf("mbc", [128, 512], F32)
    c["rbc"] = kb.sbuf("rbc", [128, 512], F32)
    c["ones128"] = kb.inp("c_ones128", [128, 1], BF16)
    c["ones1f"] = kb.inp("c_ones1f", [128, 128], F32)
    c["fcol"] = kb.inp("c_fcol", [128, 8], F32)
    c["s_ones128"] = kb.sbuf("s_ones128", [128, 1], BF16)
    c["s_ones1f"] = kb.sbuf("s_ones1f", [128, 128], F32)
    c["s_fcol"] = kb.sbuf("s_fcol", [128, 8], F32)
    csem = kb.sem("consts")
    kb.dma("sync", c["s_ones128"][:, :], c["ones128"][:, :], csem)
    kb.dma("sync", c["s_ones1f"][:, :], c["ones1f"][:, :], csem)
    kb.dma("sync", c["s_fcol"][:, :], c["fcol"][:, :], csem)
    c["csem"] = csem
    c["PB"] = [kb.psum(f"pb{i}", [128, 512], F32) for i in range(8)]
    kb.flt = kb.sbuf("flt", [1, 16], F32)
    kb.E("vector", "memzero", kb.flt[0:1, 0:8])
    kb.E("scalar", "memzero", kb.flt[0:1, 8:16])
    return c


def emit_stage_x(kb, c, src_dram, sfx):
    dsem = [kb.sem(f"stgx{i}" + sfx) for i in range(4)]
    ssem = kb.sem("stgxs" + sfx)
    buf = c["PF"][2]
    for dt in range(ND):
        for ch in range(NCH):
            k = dt * NCH + ch
            s = k % 4
            slot = buf[:, s * 512:(s + 1) * 512]
            if k >= 4:
                kb.wait("sync", ssem, k - 3)
            kb.dma("sync", slot,
                   src_dram[dt * 128:(dt + 1) * 128,
                            ch * 512:(ch + 1) * 512], dsem[s])
            kb.wait("scalar", dsem[s], dsem[s].n)
            kb.E("scalar", "copy",
                 c["xln"][:, dt * T + ch * 512: dt * T + (ch + 1) * 512],
                 slot, sem=ssem)
    kb.sfl(ssem)
    return ssem, ssem.n


def emit_ln(kb, c, sfx, ready_sem, ready_tgt):
    PB = c["PB"]
    xln = c["xln"]
    st = c["strow"]
    lnt = kb.sem("lnt" + sfx)
    lnv = kb.sem("lnv" + sfx)
    lns = kb.sem("lns" + sfx)
    lnt0 = lnt.n

    for b in range(8):
        kb.bank_wait(b)
    kb.wait("tensor", ready_sem, ready_tgt)
    kb.wait("tensor", c["csem"], c["csem"].n)
    kb.wait("scalar", c["csem"], c["csem"].n)
    kb.wait("vector", c["csem"], c["csem"].n)

    for dt in range(ND):
        sqb = c["PO"][dt % 2]
        if dt >= 2:
            kb.wait("scalar", lnt, lnt0 + dt - 1)
        kb.wait("scalar", ready_sem, ready_tgt)
        kb.E("scalar", "activation", sqb[:, :], xln[:, dt * T:(dt + 1) * T],
             AF.Square)
        kb.sfl(lns)
        kb.wait("tensor", lns, lns.n)
        for ch in range(NCH):
            kb.E("tensor", "matmul", PB[ch][0:1, :],
                 c["s_ones128"][:, :],
                 xln[:, dt * T + ch * 512: dt * T + (ch + 1) * 512],
                 start=(dt == 0), stop=(dt == ND - 1))
            kb.E("tensor", "matmul", PB[4 + ch][0:1, :],
                 c["s_ones128"][:, :], sqb[:, ch * 512:(ch + 1) * 512],
                 start=(dt == 0), stop=(dt == ND - 1),
                 sem=lnt if ch == NCH - 1 else None)

    xtmp = c["PF"][3][:, 0:512]
    mrow = st[0:1, 0:512]
    vrow = st[0:1, 512:1024]
    trow = st[0:1, 1024:1536]
    kb.wait("vector", lnt, lnt0 + ND)
    for ch in range(NCH):
        kb.E("vector", "tensor_scalar_mul", mrow, PB[ch][0:1, :], 1.0 / D)
        kb.E("vector", "tensor_scalar_mul", vrow, PB[4 + ch][0:1, :], 1.0 / D)
        kb.E("vector", "tensor_mul", trow, mrow, mrow)
        kb.E("vector", "tensor_sub", vrow, vrow, trow)
        kb.vfl(lnv)
        kb.wait("scalar", lnv, lnv.n)
        kb.E("scalar", "activation", trow, vrow, AF.Ln, bias=fc(c, FC_EPS, 1))
        kb.E("scalar", "activation", vrow, trow, AF.Exp, scale=-0.5)
        kb.sfl(lns)
        kb.wait("tensor", lnv, lnv.n)
        kb.wait("tensor", lns, lns.n)
        kb.E("tensor", "matmul", PB[ch][:, :], c["s_ones1f"][0:1, :],
             mrow, start=True, stop=True)
        kb.E("tensor", "matmul", PB[4 + ch][:, :], c["s_ones1f"][0:1, :],
             vrow, start=True, stop=True, sem=lnt)
        kb.wait("vector", lnt, lnt.n)
        for _ in range(8):
            kb.vfl(lnv)
        kb.E("vector", "tensor_copy", c["mbc"][:, :], PB[ch][:, :])
        kb.E("vector", "tensor_copy", c["rbc"][:, :], PB[4 + ch][:, :],
             sem=lnv)
        kb.bank_set(ch, lnv, lnv.n)
        kb.bank_set(4 + ch, lnv, lnv.n)
        for dt in range(ND):
            a0 = dt * T + ch * 512
            kb.E("vector", "tensor_sub", xtmp, xln[:, a0:a0 + 512],
                 c["mbc"][:, :])
            kb.E("vector", "tensor_mul", xln[:, a0:a0 + 512], xtmp,
                 c["rbc"][:, :], sem=lnv)
    kb.vfl(lnv)
    return lnv, lnv.n



def emit_l1ffn(kb, c, sfx, wgate_b, wup_b, wdown_b, inter_dram,
               xsrc, xdst, wready, ready, scale=0.5):
    PB = c["PB"]
    xln = c["xln"]
    PW, PF, PO = c["PW"], c["PF"], c["PO"]
    NJ = 16

    wsem = [kb.sem("l1f_w0" + sfx), kb.sem("l1f_w1" + sfx)]
    msem = kb.sem("l1f_m" + sfx)
    vsem = kb.sem("l1f_v" + sfx)
    ssem = kb.sem("l1f_s" + sfx)
    isem = kb.sem("l1f_i" + sfx)
    isl = [kb.sem("l1f_ia" + sfx), kb.sem("l1f_ib" + sfx)]
    desm = [kb.sem("l1f_d0" + sfx), kb.sem("l1f_d1" + sfx)]
    xsem = kb.sem("l1f_x" + sfx)
    osem = kb.sem("l1f_o" + sfx)
    epv = kb.sem("l1f_e" + sfx)
    mmB = kb.sem("l1f_mb" + sfx)
    msem0, vsem0, epv0 = msem.n, vsem.n, epv.n

    wg_sb = [PW[0], PW[1]]
    wu_sb = [PW[2], PW[3]]
    sg, it, ab, tp = PF[0], PF[1], PF[2], PF[3]
    ob = [PO[0], PO[1]]

    for (s, t) in wready:
        kb.wait("sync", s, t)
    for eng in ("tensor", "scalar", "vector"):
        kb.wait(eng, ready[0], ready[1])

    for j in range(NJ):
        bsel = j % 2
        if j >= 2:
            kb.wait("sync", msem, msem0 + 32 * (j - 1))
        kb.dma("sync",
               wg_sb[bsel][:, :].rearrange("p (h w) -> p h w", h=16),
               wgate_b[:, j * 512:(j + 1) * 512].rearrange(
                   "(h p) w -> p h w", p=128), wsem[bsel])
        kb.dma("sync",
               wu_sb[bsel][:, :].rearrange("p (h w) -> p h w", h=16),
               wup_b[:, j * 512:(j + 1) * 512].rearrange(
                   "(h p) w -> p h w", p=128), wsem[bsel])
        kb.wait("tensor", wsem[bsel], wsem[bsel].n)
        for ch in range(NCH):
            for r in range(4):
                kb.bank_wait(2 * r)
                kb.bank_wait(2 * r + 1)
                wsb, usb = wg_sb[bsel], wu_sb[bsel]
                for ht in range(16):
                    lhs_g = wsb[:, ht * 512 + r: ht * 512 + 512: 4]
                    lhs_u = usb[:, ht * 512 + r: ht * 512 + 512: 4]
                    rhs = xln[:, ht * T + ch * 512: ht * T + (ch + 1) * 512]
                    kb.E("tensor", "matmul", PB[2 * r][:, :], lhs_g, rhs,
                         start=(ht == 0), stop=(ht == 15),
                         sem=msem if ht == 15 else None)
                    kb.E("tensor", "matmul", PB[2 * r + 1][:, :], lhs_u, rhs,
                         start=(ht == 0), stop=(ht == 15),
                         sem=msem if ht == 15 else None)
            kb.wait("scalar", vsem, max(vsem0, vsem.n - 4))
            for r in range(4):
                kb.wait("scalar", msem, msem.n)
                kb.E("scalar", "activation", sg[:, r * 512:(r + 1) * 512],
                     PB[2 * r][:, :], AF.Silu, sem=ssem)
            kb.sfl(ssem)
            for r in range(4):
                kb.wait("vector", ssem, ssem.n)
                kb.wait("vector", msem, msem.n)
                kb.E("vector", "tensor_mul", it[:, r * 512:(r + 1) * 512],
                     PB[2 * r + 1][:, :], sg[:, r * 512:(r + 1) * 512],
                     sem=vsem)
            for r in range(4):
                kb.bank_set(2 * r, ssem, ssem.n)
                kb.bank_set(2 * r + 1, vsem, vsem.n - 3 + r)
            for r in range(4):
                kb.E("vector", "scalar_tensor_tensor",
                     ab[:, r * 512:(r + 1) * 512],
                     it[:, r * 512:(r + 1) * 512], -1.0,
                     it[:, r * 512:(r + 1) * 512], ALU.mult, ALU.max)
            A = [ab[:, r * 512:(r + 1) * 512] for r in range(4)]
            Tp = [tp[:, r * 512:(r + 1) * 512] for r in range(4)]
            kb.E("vector", "tensor_max", Tp[0], A[0], A[1])
            kb.E("vector", "tensor_tensor", Tp[1], A[0], A[1], ALU.min)
            kb.E("vector", "tensor_max", Tp[2], A[2], A[3])
            kb.E("vector", "tensor_tensor", Tp[3], A[2], A[3], ALU.min)
            kb.E("vector", "tensor_tensor", Tp[0], Tp[0], Tp[2], ALU.min)
            kb.E("vector", "tensor_max", Tp[1], Tp[1], Tp[3])
            kb.E("vector", "tensor_max", Tp[0], Tp[0], Tp[1])
            k_glob = j * NCH + ch
            obuf = ob[k_glob % 2]
            if k_glob >= 2:
                kb.wait("vector", isl[k_glob % 2], isl[k_glob % 2].n)
            for r in range(4):
                kb.E("vector", "tensor_tensor", Tp[1], A[r], Tp[0],
                     ALU.is_ge)
                kb.E("vector", "tensor_mul", obuf[:, r * 512:(r + 1) * 512],
                     it[:, r * 512:(r + 1) * 512], Tp[1], sem=vsem)
            kb.vfl(vsem)
            kb.wait("sync", vsem, vsem.n)
            kb.dma("sync",
                   inter_dram[j * 512:(j + 1) * 512,
                              ch * 512:(ch + 1) * 512].rearrange(
                       "(r p) w -> p r w", p=128),
                   obuf[:, :].rearrange("p (r w) -> p r w", r=4),
                   isl[k_glob % 2])
            isem.inc(0)

    wd_sb = [PW[0], PW[1]]
    itb = [PW[2], PW[3]]

    def xo_slot(ht):
        return (PF[0] if ht < 4 else PF[1])[:, (ht % 4) * 512:
                                            (ht % 4 + 1) * 512]

    def xn_slot(ht):
        return (PF[2] if ht < 4 else PF[3])[:, (ht % 4) * 512:
                                            (ht % 4 + 1) * 512]

    kb.wait("sync", isl[0], isl[0].n)
    kb.wait("sync", isl[1], isl[1].n)
    for ch in range(NCH):
        for hs in range(2):
            k_iter = ch * 2 + hs
            if k_iter >= 1:
                kb.wait("sync", epv, epv0 + 8 * k_iter)
            for ht in range(8):
                dtile = hs * 8 + ht
                kb.dma("sync", xo_slot(ht),
                       xsrc[dtile * 128:(dtile + 1) * 128,
                            ch * 512:(ch + 1) * 512], xsem)
            for b in range(8):
                kb.bank_wait(b)
            for j in range(NJ):
                bsel = j % 2
                if j >= 2:
                    kb.wait("sync", mmB, mmB.n - 4)
                kb.dma("sync",
                       wd_sb[bsel][:, 0:4096].rearrange(
                           "p (r w) -> p r w", r=4),
                       wdown_b[j * 512:(j + 1) * 512,
                               hs * 1024:(hs + 1) * 1024].rearrange(
                           "(p r) w -> p r w", r=4), desm[bsel])
                kb.dma("sync",
                       itb[bsel][:, 0:2048].rearrange("p (r w) -> p r w", r=4),
                       inter_dram[j * 512:(j + 1) * 512,
                                  ch * 512:(ch + 1) * 512].rearrange(
                           "(r p) w -> p r w", p=128), desm[bsel])
                kb.wait("tensor", desm[bsel], desm[bsel].n)
                for r in range(4):
                    for ht in range(8):
                        kb.E("tensor", "matmul", PB[ht][:, :],
                             wd_sb[bsel][:, r * 1024 + ht * 128:
                                         r * 1024 + (ht + 1) * 128],
                             itb[bsel][:, r * 512:(r + 1) * 512],
                             start=(j == 0 and r == 0),
                             stop=(j == NJ - 1 and r == 3),
                             sem=mmB if ht == 7 else None)
            kb.wait("vector", mmB, mmB.n)
            kb.wait("vector", xsem, xsem.n)
            kb.wait("vector", osem, osem.n)
            kb.wait("vector", ssem, ssem.n)
            for ht in range(8):
                kb.E("vector", "scalar_tensor_tensor", xn_slot(ht),
                     PB[ht][:, :], scale, xo_slot(ht), ALU.mult, ALU.add,
                     sem=epv)
                kb.bank_set(ht, epv, epv.n)
            kb.vfl(epv)
            kb.wait("sync", epv, epv.n)
            kb.wait("scalar", epv, epv.n)
            for ht in range(8):
                dtile = hs * 8 + ht
                kb.dma("sync", xdst[dtile * 128:(dtile + 1) * 128,
                                    ch * 512:(ch + 1) * 512],
                       xn_slot(ht), osem)
                kb.E("scalar", "copy",
                     xln[:, dtile * T + ch * 512: dtile * T + (ch + 1) * 512],
                     xn_slot(ht), sem=ssem)
    kb.sfl(ssem)
    return (ssem, ssem.n), (osem, osem.n)


class Prep:

    def __init__(self, kb, pf=None, po=None):
        self.kb = kb
        self.pf = pf if pf is not None else [
            kb.sbuf(f"prep_f{i}", [128, 1024], F32) for i in range(2)]
        self.po = po if po is not None else [
            kb.sbuf(f"prep_o{i}", [128, 1024], BF16) for i in range(2)]
        self.ld = [kb.sem("prep_ld0"), kb.sem("prep_ld1")]
        self.cv = kb.sem("prep_cv")
        self.st = [kb.sem("prep_st0"), kb.sem("prep_st1")]
        self.k = 0

    def cast(self, name, src_ap, shape):
        kb = self.kb
        dst = kb.scratch(name, shape, BF16)
        rows, cols = shape
        for r0 in range(0, rows, 128):
            rr = min(128, rows - r0)
            for c0 in range(0, cols, 1024):
                cc = min(1024, cols - c0)
                s = self.k % 2
                if self.k >= 2:
                    kb.wait("sync", self.cv, self.k - 1)
                kb.dma("sync", self.pf[s][0:rr, 0:cc],
                       src_ap[r0:r0 + rr, c0:c0 + cc], self.ld[s])
                kb.wait("vector", self.ld[s], self.ld[s].n)
                if self.k >= 2:
                    kb.wait("vector", self.st[s], self.st[s].n)
                kb.E("vector", "tensor_copy", self.po[s][0:rr, 0:cc],
                     self.pf[s][0:rr, 0:cc])
                kb.vfl(self.cv)
                kb.wait("sync", self.cv, self.cv.n)
                kb.dma("sync", dst[r0:r0 + rr, c0:c0 + cc],
                       self.po[s][0:rr, 0:cc], self.st[s])
                self.k += 1
        return dst

    def done(self):
        ret = []
        for s in (0, 1):
            if self.st[s].n:
                ret.append((self.st[s], self.st[s].n))
        return ret



def host_consts():
    import ml_dtypes
    ones128 = np.ones((128, 1), dtype=ml_dtypes.bfloat16)
    ones1f = np.ones((128, 128), dtype=np.float32)
    fcol = np.zeros((128, 8), dtype=np.float32)
    fcol[:, FC_ONE] = 1.0
    fcol[:, FC_EPS] = EPS
    fcol[:, FC_HALF] = 0.5
    fcol[:, FC_M75] = -7.5
    fcol[:, FC_P75] = 7.5
    return dict(c_ones128=ones128, c_ones1f=ones1f, c_fcol=fcol)


def kernel(**inputs):
    raise NotImplementedError("assembled incrementally; see side tests")



def emit_outproj(kb, c, sfx, w_b, act_dram, xsrc, xdst, scale, bias_pp,
                 aready):
    PB = c["PB"]
    xln = c["xln"]
    PW, PF = c["PW"], c["PF"]
    wot = [PW[0], PW[1]]
    yft = PW[3]
    wosem = [kb.sem("op_w0" + sfx), kb.sem("op_w1" + sfx)]
    yftsem = kb.sem("op_yf" + sfx)
    xsem = kb.sem("op_x" + sfx)
    osem = kb.sem("op_o" + sfx)
    epv = kb.sem("op_e" + sfx)
    ssem = kb.sem("op_s" + sfx)
    mmB = kb.sem("op_m" + sfx)

    def xo_slot(ht):
        return (PF[0] if ht < 4 else PF[1])[:, (ht % 4) * 512:
                                            (ht % 4 + 1) * 512]

    def xn_slot(ht):
        return (PF[2] if ht < 4 else PF[3])[:, (ht % 4) * 512:
                                            (ht % 4 + 1) * 512]

    for (s, t) in aready:
        kb.wait("sync", s, t)
    for ch in range(NCH):
        if ch >= 1:
            kb.wait("sync", mmB, mmB.n)
        kb.dma("sync", yft[:, :].rearrange("p (kt w) -> p kt w", kt=ND),
               act_dram[:, ch * 512:(ch + 1) * 512].rearrange(
                   "(kt p) w -> p kt w", p=128), yftsem)
        kb.wait("tensor", yftsem, yftsem.n)
        for hs in range(2):
            k_iter = ch * 2 + hs
            if k_iter >= 1:
                kb.wait("sync", epv, epv.n)
            for ht in range(8):
                dtile = hs * 8 + ht
                kb.dma("sync", xo_slot(ht),
                       xsrc[dtile * 128:(dtile + 1) * 128,
                            ch * 512:(ch + 1) * 512], xsem)
            for b in range(8):
                kb.bank_wait(b)
            for kt in range(ND):
                bsel = kt % 2
                if kt >= 2:
                    kb.wait("sync", mmB, mmB.n - 1)
                kb.dma("sync", wot[bsel][:, 0:1024],
                       w_b[kt * 128:(kt + 1) * 128,
                           hs * 1024:(hs + 1) * 1024], wosem[bsel])
                kb.wait("tensor", wosem[bsel], wosem[bsel].n)
                for ht in range(8):
                    kb.E("tensor", "matmul", PB[ht][:, :],
                         wot[bsel][:, ht * 128:(ht + 1) * 128],
                         yft[:, kt * 512:(kt + 1) * 512],
                         start=(kt == 0), stop=(kt == ND - 1),
                         sem=mmB if ht == 7 else None)
            kb.wait("vector", mmB, mmB.n)
            kb.wait("vector", xsem, xsem.n)
            kb.wait("vector", osem, osem.n)
            kb.wait("vector", ssem, ssem.n)
            for ht in range(8):
                dtile = hs * 8 + ht
                if bias_pp is None:
                    kb.E("vector", "scalar_tensor_tensor", xn_slot(ht),
                         PB[ht][:, :], scale, xo_slot(ht), ALU.mult,
                         ALU.add, sem=epv)
                else:
                    kb.E("vector", "scalar_tensor_tensor", xn_slot(ht),
                         PB[ht][:, :], bias_pp[:, dtile:dtile + 1],
                         xo_slot(ht), ALU.add, ALU.add, sem=epv)
                kb.bank_set(ht, epv, epv.n)
            kb.vfl(epv)
            kb.wait("sync", epv, epv.n)
            kb.wait("scalar", epv, epv.n)
            for ht in range(8):
                dtile = hs * 8 + ht
                kb.dma("sync", xdst[dtile * 128:(dtile + 1) * 128,
                                    ch * 512:(ch + 1) * 512],
                       xn_slot(ht), osem)
                kb.E("scalar", "copy",
                     xln[:, dtile * T + ch * 512: dtile * T + (ch + 1) * 512],
                     xn_slot(ht), sem=ssem)
    kb.sfl(ssem)
    return (ssem, ssem.n), (osem, osem.n)



def emit_l1attn(kb, c, sfx, wx_b, wg1_b, wo1_b, bgpp_d, at_d, dppp_d,
                onehot_d, id128_d, ybsel_d, gate_dram, y_dram, yf_dram,
                xsrc, xdst, wready, ready, bufs=None):
    PB = c["PB"]
    xln = c["xln"]
    PW, PF, PO = c["PW"], c["PF"], c["PO"]

    csem = kb.sem("l1a_c" + sfx)
    wsem = [kb.sem("l1a_w0" + sfx), kb.sem("l1a_w1" + sfx)]
    msem = kb.sem("l1a_m" + sfx)
    vsem = kb.sem("l1a_v" + sfx)
    ssem = kb.sem("l1a_s" + sfx)
    gsem = kb.sem("l1a_g" + sfx)
    ysem = kb.sem("l1a_y" + sfx)
    yfsm = kb.sem("l1a_yf" + sfx)
    desm = kb.sem("l1a_d" + sfx)
    xsem = kb.sem("l1a_x" + sfx)
    osem = kb.sem("l1a_o" + sfx)
    epv = kb.sem("l1a_e" + sfx)
    mmB = kb.sem("l1a_mb" + sfx)

    wx_sb = PW[3][:, 0:16 * 288]
    oh_sb = PW[0][0:32, 0:32 * 128]
    id_sb = PW[0][:, 4096:4224]
    ytmp = PW[0][0:1, 0:4 * T]
    yrow = PW[1][0:1, 0:T]
    if bufs is None:
        at_sb = kb.sbuf("l1a_at" + sfx, [128, 32], F32)
        dppp_sb = kb.sbuf("l1a_dp" + sfx, [128, 16], F32)
        bgpp_sb = kb.sbuf("l1a_bg" + sfx, [128, 16], F32)
        yb_sb = kb.sbuf("l1a_yb" + sfx, [1, 4 * 128], BF16)
        db_sb = kb.sbuf("l1a_db" + sfx, [32, T], BF16)
    else:
        at_sb, dppp_sb, bgpp_sb, yb_sb, db_sb = (
            bufs["at"], bufs["dp"], bufs["bg"], bufs["yb"], bufs["db"])

    for (s, t) in wready:
        kb.wait("sync", s, t)
    for eng in ("tensor", "scalar", "vector"):
        kb.wait(eng, ready[0], ready[1])
    kb.dma("sync", wx_sb.rearrange("p (dt f) -> p dt f", dt=ND),
           wx_b[:, :].rearrange("(dt p) f -> p dt f", p=128), csem)
    kb.dma("sync", at_sb[:, :], at_d[:, :], csem)
    kb.dma("sync", dppp_sb[:, :], dppp_d[:, :], csem)
    kb.dma("sync", bgpp_sb[:, :], bgpp_d[:, :], csem)
    kb.dma("sync", yb_sb[:, :], ybsel_d[:, :], csem)
    kb.wait("tensor", csem, csem.n)

    B_T, C_T = PO[0], PO[1]
    for fi, dst in ((0, B_T), (1, C_T)):
        for ch in range(NCH):
            b = ch % 2
            kb.bank_wait(b)
            for dt in range(ND):
                kb.E("tensor", "matmul", PB[b][:, :],
                     wx_sb[:, dt * 288 + fi * 128: dt * 288 + (fi + 1) * 128],
                     xln[:, dt * T + ch * 512: dt * T + (ch + 1) * 512],
                     start=(dt == 0), stop=(dt == ND - 1),
                     sem=msem if dt == ND - 1 else None)
            kb.wait("scalar", msem, msem.n)
            kb.E("scalar", "copy", dst[:, ch * 512:(ch + 1) * 512],
                 PB[b][:, :], sem=ssem)
            kb.bank_set(b, ssem, ssem.n)
    for ch in range(NCH):
        b = 2 + ch % 2
        kb.bank_wait(b)
        for dt in range(ND):
            kb.E("tensor", "matmul", PB[b][0:32, :],
                 wx_sb[:, dt * 288 + 256: dt * 288 + 288],
                 xln[:, dt * T + ch * 512: dt * T + (ch + 1) * 512],
                 start=(dt == 0), stop=(dt == ND - 1),
                 sem=msem if dt == ND - 1 else None)
        kb.wait("scalar", msem, msem.n)
        kb.E("scalar", "copy", db_sb[:, ch * 512:(ch + 1) * 512],
             PB[b][0:32, :], sem=ssem)
        kb.bank_set(b, ssem, ssem.n)

    wgt = [PW[0], PW[1]]
    gstage = PW[2]
    for dmt in range(ND):
        b = dmt % 2
        if dmt >= 2:
            kb.wait("sync", msem, msem.n - 4)
        kb.dma("sync", wgt[b][:, 0:2048].rearrange("p (dt w) -> p dt w", dt=ND),
               wg1_b[:, dmt * 128:(dmt + 1) * 128].rearrange(
                   "(dt p) w -> p dt w", p=128), wsem[b])
        kb.wait("tensor", wsem[b], wsem[b].n)
        for ch in range(NCH):
            pb = 4 + ch
            kb.bank_wait(pb)
            for dt in range(ND):
                kb.E("tensor", "matmul", PB[pb][:, :],
                     wgt[b][:, dt * 128:(dt + 1) * 128],
                     xln[:, dt * T + ch * 512: dt * T + (ch + 1) * 512],
                     start=(dt == 0), stop=(dt == ND - 1),
                     sem=msem if dt == ND - 1 else None)
        kb.wait("scalar", msem, msem.n)
        if dmt >= 2:
            kb.wait("scalar", gsem, gsem.n)
        for ch in range(NCH):
            pb = 4 + ch
            kb.E("scalar", "activation",
                 gstage[:, (ch if dmt % 2 == 0 else 4 + ch) * 512:
                        (ch if dmt % 2 == 0 else 4 + ch) * 512 + 512],
                 PB[pb][:, :], AF.Sigmoid, bias=bgpp_sb[:, dmt:dmt + 1],
                 sem=ssem)
            kb.bank_set(pb, ssem, ssem.n)
        kb.sfl(ssem)
        kb.wait("sync", ssem, ssem.n)
        off = 0 if dmt % 2 == 0 else 4 * 512
        kb.dma("sync", gate_dram[dmt * 128:(dmt + 1) * 128, :],
               gstage[:, off:off + 2048], gsem)

    kb.wait("sync", msem, msem.n)
    csem2 = kb.sem("l1a_c2" + sfx)
    kb.dma("sync", oh_sb, onehot_d[:, :], csem2)
    kb.dma("sync", id_sb, id128_d[:, :], csem2)
    kb.wait("tensor", csem2, csem2.n)
    a_buf, u_buf, hs_buf, tmpf = PF[0], PF[1], PF[2], PF[3]
    P_bf = PW[3]
    for h in range(32):
        for ch in range(NCH):
            kb.bank_wait(ch)
            kb.E("tensor", "matmul", PB[ch][:, :], id_sb,
                 B_T[:, ch * 512:(ch + 1) * 512], start=True, stop=False)
            kb.E("tensor", "matmul", PB[ch][:, :],
                 oh_sb[:, h * 128:(h + 1) * 128],
                 db_sb[:, ch * 512:(ch + 1) * 512], start=False, stop=True,
                 sem=msem)
        kb.wait("scalar", msem, msem.n)
        kb.wait("scalar", vsem, vsem.n)
        for ch in range(NCH):
            s0 = ch * 512
            kb.E("scalar", "activation", tmpf[:, s0:s0 + 512], PB[ch][:, :],
                 AF.Exp)
            kb.E("scalar", "activation", tmpf[:, s0:s0 + 512],
                 tmpf[:, s0:s0 + 512], AF.Ln, bias=fc(c, FC_ONE))
            kb.E("scalar", "activation", u_buf[:, s0:s0 + 512],
                 tmpf[:, s0:s0 + 512], AF.Square, sem=ssem)
            kb.bank_set(ch, ssem, ssem.n)
        kb.sfl(ssem)
        kb.wait("vector", ssem, ssem.n)
        kb.E("vector", "tensor_scalar_mul", a_buf[:, :], tmpf[:, :],
             at_sb[:, h:h + 1], sem=vsem)
        kb.vfl(vsem)
        kb.wait("scalar", vsem, vsem.n)
        kb.E("scalar", "activation", a_buf[:, :], a_buf[:, :], AF.Exp)
        kb.sfl(ssem)
        kb.wait("vector", ssem, ssem.n)
        kb.E("vector", "tensor_tensor_scan", hs_buf[:, :], a_buf[:, :],
             u_buf[:, :], 0.0, ALU.mult, ALU.add)
        kb.E("vector", "tensor_mul", P_bf[:, 0:T], hs_buf[:, :], C_T[:, :])
        kb.vfl(vsem)
        kb.wait("tensor", vsem, vsem.n)
        kb.wait("tensor", c["csem"], c["csem"].n)
        for ch in range(NCH):
            pb = 4 + ch
            kb.bank_wait(pb)
            kb.E("tensor", "matmul", PB[pb][0:1, :], c["s_ones128"][:, :],
                 P_bf[:, ch * 512:(ch + 1) * 512], start=True, stop=True,
                 sem=msem if ch == NCH - 1 else None)
        kb.wait("scalar", msem, msem.n)
        if h >= 1:
            kb.wait("scalar", ysem, ysem.n)
        for ch in range(NCH):
            pb = 4 + ch
            kb.E("scalar", "copy", yrow[0:1, ch * 512:(ch + 1) * 512],
                 PB[pb][0:1, :], sem=ssem)
            kb.bank_set(pb, ssem, ssem.n)
        kb.sfl(ssem)
        kb.wait("sync", ssem, ssem.n)
        kb.dma("sync", y_dram[h:h + 1, :], yrow, ysem)

    gtile = PW[2]
    yfst = [PF[0], PF[1]]
    kb.wait("sync", ysem, ysem.n)
    for dmt in range(ND):
        if dmt >= 1:
            kb.wait("sync", msem, msem.n)
            kb.wait("sync", yfsm, yfsm.n)
        kb.dma("sync", ytmp,
               y_dram[4 * (dmt % 8): 4 * (dmt % 8) + 4, :].rearrange(
                   "(o a) t -> o (a t)", o=1), desm)
        kb.dma("sync", gtile[:, (dmt % 2) * 2048:(dmt % 2) * 2048 + 2048],
               gate_dram[dmt * 128:(dmt + 1) * 128, :], desm)
        kb.wait("tensor", desm, desm.n)
        kb.wait("vector", yfsm, yfsm.n)
        for ch in range(NCH):
            pb = ch % 2
            kb.bank_wait(pb)
            for s in range(4):
                kb.E("tensor", "matmul", PB[pb][:, :],
                     yb_sb[0:1, s * 128:(s + 1) * 128],
                     ytmp[0:1, s * T + ch * 512: s * T + (ch + 1) * 512],
                     start=(s == 0), stop=(s == 3),
                     sem=msem if s == 3 else None)
            kb.wait("vector", msem, msem.n)
            yslot = yfst[ch % 2][:, (ch // 2) * 512:(ch // 2) * 512 + 512]
            kb.E("vector", "tensor_scalar_mul", yslot,
                 xln[:, dmt * T + ch * 512: dmt * T + (ch + 1) * 512],
                 dppp_sb[:, dmt:dmt + 1])
            kb.E("vector", "tensor_add", yslot, yslot, PB[pb][:, :],
                 sem=vsem)
            kb.bank_set(pb, vsem, vsem.n)
            kb.E("vector", "tensor_mul",
                 P_bf[:, ch * 512:(ch + 1) * 512], yslot,
                 gtile[:, (dmt % 2) * 2048 + ch * 512:
                       (dmt % 2) * 2048 + (ch + 1) * 512], sem=vsem)
        kb.vfl(vsem)
        kb.wait("sync", vsem, vsem.n)
        kb.dma("sync", yf_dram[dmt * 128:(dmt + 1) * 128, :], P_bf[:, 0:T],
               yfsm)

    st, ot = emit_outproj(kb, c, sfx + "op", wo1_b, yf_dram, xsrc, xdst,
                          1.0, None, [(yfsm, yfsm.n)])
    return st, ot


def host_l1attn_inputs(Wx, Wg, bg, Wo, A, Dp):
    import ml_dtypes
    d = {}
    d["l1a_wx"] = np.ascontiguousarray(Wx[SIGMA, :])
    d["l1a_wg"] = np.ascontiguousarray(Wg[SIGMA][:, SIGMA])
    d["l1a_wo"] = np.ascontiguousarray(Wo[SIGMA][:, SIGMA])
    d["l1a_bg"] = np.ascontiguousarray(bg[SIGMA].reshape(16, 128).T)
    d["l1a_at"] = np.ascontiguousarray(A.T.astype(np.float32))
    d["l1a_dp"] = np.ascontiguousarray(
        Dp[SIGMA // 64].reshape(16, 128).T.astype(np.float32))
    oh = np.zeros((32, 32 * 128), dtype=ml_dtypes.bfloat16)
    for h in range(32):
        oh[h, h * 128:(h + 1) * 128] = 1.0
    d["l1a_oh"] = oh
    d["l1a_id"] = np.eye(128, dtype=ml_dtypes.bfloat16)
    yb = np.zeros((1, 4 * 128), dtype=ml_dtypes.bfloat16)
    for s in range(4):
        yb[0, s * 128 + s * 32: s * 128 + (s + 1) * 32] = 1.0
    d["l1a_yb"] = yb
    return d


def dummy_mm(kb, c, scrap_bank, msem):
    kb.bank_wait(scrap_bank)
    kb.E("tensor", "matmul", c["PB"][scrap_bank][0:1, :],
         c["s_ones128"][0:1, 0:1], c["xln"][0:1, 0:512],
         start=True, stop=True, sem=msem)
    kb.bank_set(scrap_bank, msem, msem.n)
    return msem.n


def emit_l0attn(kb, c, sfx, wqkv_tb, wout_tb, bqk_d, bvbc_d, boutpp_d,
                alibi_d, cmask_d, ctx_dram, xsrc, xdst, wready, ready,
                bufs=None):
    PB = c["PB"]
    xln = c["xln"]
    PW, PF, PO = c["PW"], c["PF"], c["PO"]
    SC = 1.0 / float(np.sqrt(128.0))

    csem = kb.sem("l0a_c" + sfx)
    vwsem = kb.sem("l0a_vw" + sfx)
    wqks = [kb.sem("l0a_w0" + sfx), kb.sem("l0a_w1" + sfx)]
    msem = kb.sem("l0a_m" + sfx)
    vsem = kb.sem("l0a_v" + sfx)
    ssem = kb.sem("l0a_s" + sfx)
    csp = [kb.sem("l0a_cs0" + sfx), kb.sem("l0a_cs1" + sfx)]

    if bufs is None:
        bqk_sb = kb.sbuf("l0a_bqk" + sfx, [128, 32], F32)
        bv_sb = kb.sbuf("l0a_bv" + sfx, [128, T], F32)
        bout_sb = kb.sbuf("l0a_bo" + sfx, [128, 16], F32)
        al_sb = kb.sbuf("l0a_al" + sfx, [128, 16 * 34], F32)
        cm_sb = kb.sbuf("l0a_cm" + sfx, [128, 4 * 512], BF16)
    else:
        bqk_sb, bv_sb, bout_sb, al_sb, cm_sb = (
            bufs["bqk"], bufs["bv"], bufs["bout"], bufs["al"], bufs["cm"])

    qt_sb = PW[0][:, 0:2048]
    kt_sb = PW[0][:, 2048:4096]
    vn_sb = PW[2]
    p_sb = PW[1]
    wtile = PW[3]
    rrow = PF[0][0:1, 0:512]
    rb_sb = PF[0][:, 512:1024]

    for (s, t) in wready:
        kb.wait("sync", s, t)
    for eng in ("tensor", "scalar", "vector"):
        kb.wait(eng, ready[0], ready[1])
    kb.dma("sync", bqk_sb[:, :], bqk_d[:, :], csem)
    kb.dma("sync", bv_sb[:, :], bvbc_d[:, :], csem)
    kb.dma("sync", bout_sb[:, :], boutpp_d[:, :], csem)
    kb.dma("sync", al_sb[:, :], alibi_d[:, :], csem)
    kb.dma("sync", cm_sb[:, :], cmask_d[:, :], csem)
    kb.wait("tensor", csem, csem.n)
    kb.wait("scalar", csem, csem.n)
    kb.wait("vector", csem, csem.n)

    for g in range(4):
        kb.wait("sync", msem, msem.n)
        kb.dma("sync", wtile[:, :].rearrange("p (dt w) -> p dt w", dt=ND),
               wqkv_tb[:, 4096 + g * 512: 4096 + (g + 1) * 512].rearrange(
                   "(dt p) w -> p dt w", p=128), vwsem)
        kb.wait("tensor", vwsem, vwsem.n)
        vg_tgts = []

        def v_evac(tt):
            b = tt % 4
            kb.wait("vector", msem, vg_tgts[tt] + 1)
            kb.E("vector", "scalar_tensor_tensor",
                 vn_sb[:, tt * 512:(tt + 1) * 512], PB[b][:, :], 1.0,
                 bv_sb[:, g * 512:(g + 1) * 512], ALU.mult, ALU.add,
                 sem=vsem)
            kb.bank_set(b, vsem, vsem.n)

        for tt in range(16):
            b = tt % 4
            kb.bank_wait(b)
            for dt in range(ND):
                kb.E("tensor", "matmul", PB[b][:, :],
                     xln[:, dt * T + tt * 128: dt * T + (tt + 1) * 128],
                     wtile[:, dt * 512:(dt + 1) * 512],
                     start=(dt == 0), stop=(dt == ND - 1),
                     sem=msem if dt == ND - 1 else None)
            vg_tgts.append(msem.n)
            if tt >= 1:
                v_evac(tt - 1)
        dummy_mm(kb, c, 6, msem)
        v_evac(15)
        kb.vfl(vsem)

        for hh in range(4):
            h = g * 4 + hh
            par = h % 2
            kb.wait("sync", msem, msem.n)
            kb.dma("sync",
                   wtile[:, par * 4096: par * 4096 + 2048].rearrange(
                       "p (dt w) -> p dt w", dt=ND),
                   wqkv_tb[:, h * 128:(h + 1) * 128].rearrange(
                       "(dt p) w -> p dt w", p=128), wqks[par])
            kb.dma("sync",
                   wtile[:, par * 4096 + 2048: par * 4096 + 4096].rearrange(
                       "p (dt w) -> p dt w", dt=ND),
                   wqkv_tb[:, 2048 + h * 128: 2048 + (h + 1) * 128].rearrange(
                       "(dt p) w -> p dt w", p=128), wqks[par])
            kb.wait("tensor", wqks[par], wqks[par].n)
            qk_tgts = []

            def qk_evac(i):
                which, ch, b, tgt = qk_tgts[i]
                kb.wait("scalar", msem, tgt + 1)
                dst = (qt_sb if which == 0 else kt_sb)
                kb.E("scalar", "activation", dst[:, ch * 512:(ch + 1) * 512],
                     PB[b][:, :], AF.Identity,
                     bias=bqk_sb[:, 16 * which + h: 16 * which + h + 1],
                     sem=ssem)
                kb.bank_set(b, ssem, ssem.n)

            seq = [(0, 0, ch) for ch in range(NCH)] +                   [(1, 2048, ch) for ch in range(NCH)]
            for i, (which, base, ch) in enumerate(seq):
                b = 4 + 2 * which + ch % 2
                kb.bank_wait(b)
                for dt in range(ND):
                    kb.E("tensor", "matmul", PB[b][:, :],
                         wtile[:, par * 4096 + base + dt * 128:
                               par * 4096 + base + (dt + 1) * 128],
                         xln[:, dt * T + ch * 512: dt * T + (ch + 1) * 512],
                         start=(dt == 0), stop=(dt == ND - 1),
                         sem=msem if dt == ND - 1 else None)
                qk_tgts.append((which, ch, b, msem.n))
                if i >= 1:
                    qk_evac(i - 1)
            dummy_mm(kb, c, 0, msem)
            qk_evac(7)
            kb.sfl(ssem)
            bh = 512 if h >= 4 else (256 if h >= 2 else 128)

            def p_exp(k2, qc):
                for qs in range(0, 512, bh):
                    qg = qc * 512 + qs
                    if k2 * 128 >= qg + bh:
                        kb.E("scalar", "memzero",
                             p_sb[:, k2 * 512 + qs: k2 * 512 + qs + bh])
                        continue
                    m64 = (qg + bh // 2 - k2 * 128) // 64 + 2
                    kb.E("scalar", "activation",
                         p_sb[:, k2 * 512 + qs: k2 * 512 + qs + bh],
                         PB[k2 % 4][:, qs:qs + bh], AF.Exp,
                         bias=al_sb[:, h * 34 + m64: h * 34 + m64 + 1],
                         scale=SC)
                kb.E("scalar", "copy", kb.flt[0:1, 12:16],
                     kb.flt[0:1, 8:12], sem=ssem)
                kb.bank_set(k2 % 4, ssem, ssem.n)

            for qc in range(NCH):
                nkt = 4 * qc + 4
                kb.wait("tensor", ssem, ssem.n)
                sc_tgts = []
                for kt in range(nkt):
                    b = kt % 4
                    kb.bank_wait(b)
                    kb.E("tensor", "matmul", PB[b][:, :],
                         kt_sb[:, kt * 128:(kt + 1) * 128],
                         qt_sb[:, qc * 512:(qc + 1) * 512],
                         start=True, stop=True, sem=msem)
                    sc_tgts.append(msem.n)
                    if kt >= 3:
                        k2 = kt - 3
                        kb.wait("scalar", msem, sc_tgts[k2] + 1)
                        p_exp(k2, qc)
                dummy_mm(kb, c, 7, msem)
                for k2 in range(max(0, nkt - 3), nkt):
                    kb.wait("scalar", msem, sc_tgts[k2] + 1)
                    p_exp(k2, qc)
                kb.sfl(ssem)
                kb.wait("vector", ssem, ssem.n)
                for r in range(4):
                    kt = 4 * qc + r
                    kb.E("vector", "tensor_mul",
                         p_sb[:, kt * 512:(kt + 1) * 512],
                         p_sb[:, kt * 512:(kt + 1) * 512],
                         cm_sb[:, r * 512:(r + 1) * 512], sem=vsem)
                kb.vfl(vsem)
                kb.wait("tensor", ssem, ssem.n)
                kb.wait("tensor", vsem, vsem.n)
                kb.bank_wait(4)
                kb.bank_wait(5)
                for kt in range(nkt):
                    kb.E("tensor", "matmul", PB[4][0:1, :],
                         c["s_ones128"][:, :], p_sb[:, kt * 512:(kt + 1) * 512],
                         start=(kt == 0), stop=(kt == nkt - 1),
                         sem=msem if kt == nkt - 1 else None)
                den_tgt = msem.n
                for kt in range(nkt):
                    kb.E("tensor", "matmul", PB[5][:, :],
                         vn_sb[:, kt * 512 + hh * 128: kt * 512 + (hh + 1) * 128],
                         p_sb[:, kt * 512:(kt + 1) * 512],
                         start=(kt == 0), stop=(kt == nkt - 1),
                         sem=msem if kt == nkt - 1 else None)
                ctx_tgt = msem.n
                kb.wait("vector", msem, den_tgt + 1)
                kb.E("vector", "reciprocal", rrow, PB[4][0:1, :])
                kb.vfl(vsem)
                kb.bank_set(4, vsem, vsem.n)
                kb.bank_wait(6)
                kb.wait("tensor", vsem, vsem.n)
                kb.E("tensor", "matmul", PB[6][:, :], c["s_ones1f"][0:1, :],
                     rrow, start=True, stop=True)
                kb.E("tensor", "matmul", PB[6][:, :], c["s_ones1f"][0:1, :],
                     rrow, start=True, stop=True, sem=msem)
                rb_tgt = msem.n
                kb.wait("scalar", msem, rb_tgt)
                kb.E("scalar", "copy", rb_sb, PB[6][:, :], sem=ssem)
                kb.bank_set(6, ssem, ssem.n)
                kb.sfl(ssem)
                slot = (h * NCH + qc) % 2
                kb.wait("vector", ssem, ssem.n)
                kb.wait("vector", msem, ctx_tgt + 1)
                if h * NCH + qc >= 2:
                    kb.wait("vector", csp[slot], csp[slot].n)
                kb.E("vector", "tensor_mul", PO[0][:, slot * 512:(slot + 1) * 512],
                     PB[5][:, :], rb_sb, sem=vsem)
                kb.bank_set(5, vsem, vsem.n)
                kb.vfl(vsem)
                kb.wait("sync", vsem, vsem.n)
                kb.dma("sync", ctx_dram[h * 128:(h + 1) * 128,
                                        qc * 512:(qc + 1) * 512],
                       PO[0][:, slot * 512:(slot + 1) * 512], csp[slot])

    st, ot = emit_outproj(kb, c, sfx + "op", wout_tb, ctx_dram, xsrc, xdst,
                          1.0, bout_sb,
                          [(csp[0], csp[0].n), (csp[1], csp[1].n)])
    return st, ot


def host_l0attn_inputs(Wqkv, b_qkv, Wout, b_out):
    import ml_dtypes
    d = {}
    d["l0a_wqkvt"] = np.ascontiguousarray(Wqkv.T[SIGMA, :])
    d["l0a_woutt"] = np.ascontiguousarray(Wout.T[:, SIGMA])
    d["l0a_bqk"] = np.ascontiguousarray(
        b_qkv[:4096].reshape(32, 128).T.astype(np.float32))
    d["l0a_bv"] = np.ascontiguousarray(np.broadcast_to(
        b_qkv[4096:], (128, 2048)).astype(np.float32))
    d["l0a_bout"] = np.ascontiguousarray(
        b_out[SIGMA].reshape(16, 128).T.astype(np.float32))
    slopes = 1.0 / 2.0 ** ((np.arange(16, dtype=np.float64) + 1) * 0.5)
    al = np.zeros((128, 16 * 34), dtype=np.float32)
    p = np.arange(128, dtype=np.float64)
    for h in range(16):
        for u in range(34):
            m = (u - 2) * 64
            al[:, h * 34 + u] = (slopes[h] * (p - m)).astype(np.float32)
    d["l0a_alibi"] = al
    cm = np.zeros((128, 4 * 512), dtype=ml_dtypes.bfloat16)
    for r in range(4):
        q = np.arange(512)[None, :]
        p = np.arange(128)[:, None]
        cm[:, r * 512:(r + 1) * 512] = (q >= r * 128 + p).astype(
            ml_dtypes.bfloat16)
    d["l0a_cmask"] = cm
    return d


SIGMA_N = np.concatenate([np.arange(0, 16384, 2), np.arange(1, 16384, 2)])


def emit_l0decode(kb, c, sfx, bufs=None):
    PW, PF, PO = c["PW"], c["PF"], c["PO"]
    pk_g = kb.inp("l0f_pkg", [16384, 1024], I32)
    sc_g = kb.inp("l0f_scg", [16384, 64], F32)
    pk_u = kb.inp("l0f_pku", [16384, 1024], I32)
    sc_u = kb.inp("l0f_scu", [16384, 64], F32)
    pk_d = kb.inp("l0f_pkd", [2048, 8192], I32)
    sc_d = kb.inp("l0f_scd", [2048, 512], F32)
    wg_dec = kb.scratch("wg_dec" + sfx, [16384, 2048], BF16)
    wu_dec = kb.scratch("wu_dec" + sfx, [16384, 2048], BF16)
    wd_dec = kb.scratch("wd_dec" + sfx, [2048, 16384], BF16)

    if bufs is None:
        spk = [kb.sbuf(f"dc_pk{i}" + sfx, [128, 1024], I32)
               for i in range(2)]
        lihi = kb.sbuf("dc_lh" + sfx, [128, 2048], I32)
    else:
        spk, lihi = bufs["spk"], bufs["lihi"]
    c2 = PO[0]
    m2 = PW[0][:, 0:2048]
    tt_ = PW[0][:, 2048:4096]
    l2 = PW[0][:, 4096:6144]
    aa = PW[0][:, 6144:8192]
    l3 = PW[1][:, 0:2048]
    gg = PW[1][:, 2048:4096]
    sgn = PW[1][:, 4096:6144]
    Wt = PW[2]

    ld = [kb.sem("dc_ld0" + sfx), kb.sem("dc_ld1" + sfx)]
    scl = [kb.sem("dc_sc0" + sfx), kb.sem("dc_sc1" + sfx)]
    wst = [kb.sem("dc_st0" + sfx), kb.sem("dc_st1" + sfx)]
    vsem = kb.sem("dc_v" + sfx)
    ssem = kb.sem("dc_s" + sfx)

    kb.wait("vector", c["csem"], c["csem"].n)
    kb.wait("scalar", c["csem"], c["csem"].n)

    its = []
    for r0 in range(0, 16384, 128):
        its.append((pk_g, sc_g, wg_dec, r0, 0, 1024))
    for r0 in range(0, 16384, 128):
        its.append((pk_u, sc_u, wu_dec, r0, 0, 1024))
    for r0 in range(0, 2048, 128):
        for pc in range(0, 8192, 1024):
            its.append((pk_d, sc_d, wd_dec, r0, pc, 8192))

    for k, (pk, sc, dec, r0, pc, FP) in enumerate(its):
        s = k % 2
        if k >= 2:
            kb.wait("sync", vsem, vsem.n - 2)
        kb.dma("sync", spk[s][:, :], pk[r0:r0 + 128, pc:pc + 1024], ld[s])
        kb.dma("sync", PF[0][:, s * 512: s * 512 + 64],
               sc[r0:r0 + 128, pc // 16: pc // 16 + 64], scl[s])
        kb.wait("vector", ld[s], ld[s].n)
        kb.E("vector", "tensor_single_scalar", lihi[:, 0:1024], spk[s][:, :],
             15, ALU.bitwise_and)
        kb.E("vector", "tensor_scalar", lihi[:, 1024:2048], spk[s][:, :],
             4, 15, ALU.logical_shift_right, ALU.bitwise_and)
        kb.E("vector", "tensor_copy", c2[:, :], lihi[:, :])
        kb.vfl(vsem)
        kb.wait("scalar", vsem, vsem.n)
        kb.E("scalar", "activation", sgn, c2[:, :], AF.Sign,
             bias=fc(c, FC_P75), scale=-1.0)
        kb.sfl(ssem)
        kb.E("vector", "tensor_scalar", m2, c2[:, :], 8.0, -8.0,
             ALU.is_ge, ALU.mult)
        kb.E("vector", "tensor_add", tt_, c2[:, :], m2)
        kb.E("vector", "tensor_scalar_add", l2, tt_, -2.0)
        kb.E("vector", "scalar_tensor_tensor", aa, tt_, 0.5, l2,
             ALU.mult, ALU.max)
        kb.E("vector", "tensor_scalar", l3, tt_, 2.0, -8.0,
             ALU.mult, ALU.add)
        kb.E("vector", "tensor_max", gg, aa, l3)
        kb.wait("vector", ssem, ssem.n)
        kb.E("vector", "tensor_mul", gg, gg, sgn)
        kb.wait("vector", scl[s], scl[s].n)
        kb.wait("vector", wst[s], wst[s].n)
        sc_ap = bass.AP(PF[0], s * 512, [[2048, 128], [1, 64], [0, 16]])
        for pl in range(2):
            kb.E("vector", "tensor_tensor",
                 Wt[:, (2 * s + pl) * 1024: (2 * s + pl) * 1024 + 1024].rearrange(
                     "p (a b) -> p a b", a=64),
                 gg[:, pl * 1024:(pl + 1) * 1024].rearrange(
                     "p (a b) -> p a b", a=64),
                 sc_ap, ALU.mult)
        kb.vfl(vsem)
        kb.wait("sync", vsem, vsem.n)
        if FP == 1024:
            kb.dma("sync", dec[r0:r0 + 128, :],
                   Wt[:, 2 * s * 1024: 2 * s * 1024 + 2048], wst[s])
        else:
            kb.dma("sync", dec[r0:r0 + 128, pc:pc + 1024],
                   Wt[:, 2 * s * 1024: 2 * s * 1024 + 1024], wst[s])
            kb.dma("sync", dec[r0:r0 + 128, 8192 + pc: 8192 + pc + 1024],
                   Wt[:, (2 * s + 1) * 1024: (2 * s + 1) * 1024 + 1024],
                   wst[s])
    done = [(wst[0], wst[0].n), (wst[1], wst[1].n)]
    return wg_dec, wu_dec, wd_dec, done


def host_l0ffn_inputs(gW, gS, uW, uS, dW, dS):
    return {
        "l0f_pkg": np.ascontiguousarray(gW[SIGMA_N, :]),
        "l0f_scg": np.ascontiguousarray(gS[SIGMA_N, :]),
        "l0f_pku": np.ascontiguousarray(uW[SIGMA_N, :]),
        "l0f_scu": np.ascontiguousarray(uS[SIGMA_N, :]),
        "l0f_pkd": np.ascontiguousarray(dW[SIGMA, :]),
        "l0f_scd": np.ascontiguousarray(dS[SIGMA, :]),
    }

def emit_l0ffn(kb, c, sfx, wg_dec, wu_dec, wd_dec, inter_dram,
               xsrc, xdst, wready, ready, scale=0.5):
    PB = c["PB"]
    xln = c["xln"]
    PW, PF, PO = c["PW"], c["PF"], c["PO"]
    NS = 32

    wsem = [kb.sem("l0f_w0" + sfx), kb.sem("l0f_w1" + sfx)]
    msem = kb.sem("l0f_m" + sfx)
    vsem = kb.sem("l0f_v" + sfx)
    ssem = kb.sem("l0f_s" + sfx)
    isl = [kb.sem("l0f_i0" + sfx), kb.sem("l0f_i1" + sfx)]
    desm = [kb.sem("l0f_d0" + sfx), kb.sem("l0f_d1" + sfx)]
    xsem = kb.sem("l0f_x" + sfx)
    osem = kb.sem("l0f_o" + sfx)
    epv = kb.sem("l0f_e" + sfx)
    mmB = kb.sem("l0f_mb" + sfx)
    msem0 = msem.n

    wg_sb = [PW[0], PW[1]]
    wu_sb = [PW[2], PW[3]]
    sg, up, tp = PF[0], PF[1], PF[3]
    ob = [PO[0], PO[1]]

    for (s, t) in wready:
        kb.wait("sync", s, t)
    for eng in ("tensor", "scalar", "vector"):
        kb.wait(eng, ready[0], ready[1])

    for j in range(NS):
        bsel = j % 2
        if j >= 2:
            kb.wait("sync", msem, msem0 + 32 * (j - 1))
        for kt in range(ND):
            kb.dma("sync", wg_sb[bsel][:, kt * 512:(kt + 1) * 512],
                   wg_dec[j * 512:(j + 1) * 512, kt * 128:(kt + 1) * 128],
                   wsem[bsel], transpose=True)
            kb.dma("sync", wu_sb[bsel][:, kt * 512:(kt + 1) * 512],
                   wu_dec[j * 512:(j + 1) * 512, kt * 128:(kt + 1) * 128],
                   wsem[bsel], transpose=True)
        kb.wait("tensor", wsem[bsel], wsem[bsel].n)
        for ch in range(NCH):
            for r in range(4):
                kb.bank_wait(2 * r)
                kb.bank_wait(2 * r + 1)
                for kt in range(ND):
                    lhs_g = wg_sb[bsel][:, kt * 512 + r * 128:
                                        kt * 512 + (r + 1) * 128]
                    lhs_u = wu_sb[bsel][:, kt * 512 + r * 128:
                                        kt * 512 + (r + 1) * 128]
                    rhs = xln[:, kt * T + ch * 512: kt * T + (ch + 1) * 512]
                    kb.E("tensor", "matmul", PB[2 * r][:, :], lhs_g, rhs,
                         start=(kt == 0), stop=(kt == ND - 1),
                         sem=msem if kt == ND - 1 else None)
                    kb.E("tensor", "matmul", PB[2 * r + 1][:, :], lhs_u, rhs,
                         start=(kt == 0), stop=(kt == ND - 1),
                         sem=msem if kt == ND - 1 else None)
            kb.wait("scalar", vsem, vsem.n)
            for r in range(4):
                kb.wait("scalar", msem, msem.n - 7 + 2 * r)
                kb.E("scalar", "activation", sg[:, r * 512:(r + 1) * 512],
                     PB[2 * r][:, :], AF.Identity, sem=ssem)
                kb.wait("scalar", msem, msem.n - 6 + 2 * r)
                kb.E("scalar", "activation", up[:, r * 512:(r + 1) * 512],
                     PB[2 * r + 1][:, :], AF.Identity, sem=ssem)
                kb.bank_set(2 * r, ssem, ssem.n)
                kb.bank_set(2 * r + 1, ssem, ssem.n)
            kb.sfl(ssem)
            k_glob = j * NCH + ch
            obuf = ob[k_glob % 2]
            if k_glob >= 2:
                kb.wait("vector", isl[k_glob % 2], isl[k_glob % 2].n)
            kb.wait("vector", ssem, ssem.n)
            for r in range(4):
                g0 = sg[:, r * 512:(r + 1) * 512]
                t1 = tp[:, 0:512]
                t2 = tp[:, 512:1024]
                kb.E("vector", "tensor_mul", t1, g0, g0)
                kb.E("vector", "tensor_scalar", t2, t1, 0.035677408136,
                     0.7978845608, ALU.mult, ALU.add)
                kb.E("vector", "tensor_mul", t1, g0, t2)
                kb.vfl(vsem)
                kb.wait("scalar", vsem, vsem.n)
                kb.E("scalar", "activation", t2, t1, AF.Tanh)
                kb.sfl(ssem)
                kb.wait("vector", ssem, ssem.n)
                kb.E("vector", "scalar_tensor_tensor", t1, t2, 1.0, g0,
                     ALU.add, ALU.mult)
                kb.E("vector", "scalar_tensor_tensor",
                     obuf[:, r * 512:(r + 1) * 512], t1, 0.5,
                     up[:, r * 512:(r + 1) * 512], ALU.mult, ALU.mult,
                     sem=vsem)
            kb.vfl(vsem)
            kb.wait("sync", vsem, vsem.n)
            kb.dma("sync",
                   inter_dram[j * 512:(j + 1) * 512,
                              ch * 512:(ch + 1) * 512].rearrange(
                       "(r p) w -> p r w", p=128),
                   obuf[:, :].rearrange("p (r w) -> p r w", r=4),
                   isl[k_glob % 2])

    wd_sb = [PW[0][:, 0:512], PW[0][:, 512:1024]]
    itb = [PW[2][:, 0:1024], PW[2][:, 1024:2048]]

    def xo_slot(q):
        return (PF[0] if q < 4 else PF[1])[:, (q % 4) * 512:
                                           (q % 4 + 1) * 512]

    def xn_slot(q):
        return (PF[2] if q < 4 else PF[3])[:, (q % 4) * 512:
                                           (q % 4 + 1) * 512]

    kb.wait("sync", isl[0], isl[0].n)
    kb.wait("sync", isl[1], isl[1].n)
    for tq in range(2):
        for dq in range(4):
            k_iter = tq * 4 + dq
            if k_iter >= 1:
                kb.wait("sync", epv, epv.n)
            for q in range(8):
                tile, seg = q // 2, q % 2
                kb.dma("sync", xo_slot(q),
                       xsrc[dq * 512 + tile * 128: dq * 512 + (tile + 1) * 128,
                            tq * 1024 + seg * 512: tq * 1024 + (seg + 1) * 512],
                       xsem)
            for b in range(8):
                kb.bank_wait(b)
            for ci in range(128):
                bsel = ci % 2
                if ci >= 2:
                    kb.wait("sync", mmB, mmB.n - 1)
                kb.dma("sync", wd_sb[bsel],
                       wd_dec[dq * 512:(dq + 1) * 512,
                              ci * 128:(ci + 1) * 128],
                       wsem[bsel], transpose=True)
                kb.dma("sync", itb[bsel],
                       inter_dram[ci * 128:(ci + 1) * 128,
                                  tq * 1024:(tq + 1) * 1024], desm[bsel])
                kb.wait("tensor", wsem[bsel], wsem[bsel].n)
                kb.wait("tensor", desm[bsel], desm[bsel].n)
                for q in range(8):
                    tile, seg = q // 2, q % 2
                    kb.E("tensor", "matmul", PB[q][:, :],
                         wd_sb[bsel][:, tile * 128:(tile + 1) * 128],
                         itb[bsel][:, seg * 512:(seg + 1) * 512],
                         start=(ci == 0), stop=(ci == 127),
                         sem=mmB if q == 7 else None)
            kb.wait("vector", mmB, mmB.n)
            kb.wait("vector", xsem, xsem.n)
            kb.wait("vector", osem, osem.n)
            kb.wait("vector", ssem, ssem.n)
            for q in range(8):
                kb.E("vector", "scalar_tensor_tensor", xn_slot(q),
                     PB[q][:, :], scale, xo_slot(q), ALU.mult, ALU.add,
                     sem=epv)
                kb.bank_set(q, epv, epv.n)
            kb.vfl(epv)
            kb.wait("sync", epv, epv.n)
            kb.wait("scalar", epv, epv.n)
            for q in range(8):
                tile, seg = q // 2, q % 2
                dtile = dq * 4 + tile
                tcol = tq * 1024 + seg * 512
                kb.dma("sync", xdst[dtile * 128:(dtile + 1) * 128,
                                    tcol:tcol + 512], xn_slot(q), osem)
                kb.E("scalar", "copy",
                     xln[:, dtile * T + tcol: dtile * T + tcol + 512],
                     xn_slot(q), sem=ssem)
    kb.sfl(ssem)
    return (ssem, ssem.n), (osem, osem.n)


def build_l0ffn_launch():
    nc = bass.Bass("TRN2", target_bir_lowering=False)
    es = contextlib.ExitStack()
    with es:
        kb = KB(nc, es)
        xt = kb.inp("xt", [D, T], F32)
        yt = kb.outp("yt", [D, T], F32)
        c = alloc_common(kb)
        wg_dec, wu_dec, wd_dec, dready = emit_l0decode(kb, c, "0")
        inter = kb.scratch("inter", [16384, T], BF16)
        stg = emit_stage_x(kb, c, xt, "0")
        lnr = emit_ln(kb, c, "0", *stg)
        (_, _), (ost, ost_n) = emit_l0ffn(
            kb, c, "0", wg_dec, wu_dec, wd_dec, inter, xt, yt, dready, lnr)
        kb.wait("sync", ost, ost_n)
        kb.finalize()
    return nc, kb.input_names


def build_l0attn_launch():
    nc = bass.Bass("TRN2", target_bir_lowering=False)
    es = contextlib.ExitStack()
    with es:
        kb = KB(nc, es)
        xt = kb.inp("xt", [D, T], F32)
        wqkvt = kb.inp("l0a_wqkvt", [D, 6144], F32)
        woutt = kb.inp("l0a_woutt", [D, D], F32)
        bqk = kb.inp("l0a_bqk", [128, 32], F32)
        bv = kb.inp("l0a_bv", [128, T], F32)
        bout = kb.inp("l0a_bout", [128, 16], F32)
        alibi = kb.inp("l0a_alibi", [128, 16 * 34], F32)
        cmask = kb.inp("l0a_cmask", [128, 4 * 512], BF16)
        yt = kb.outp("yt", [D, T], F32)
        c = alloc_common(kb)
        prep = Prep(kb)
        wqkv_b = prep.cast("wqkvt_b", wqkvt, [D, 6144])
        wout_b = prep.cast("woutt_b", woutt, [D, D])
        wcs = prep.done()
        ctx_dram = kb.scratch("ctx_dram", [D, T], BF16)
        stg = emit_stage_x(kb, c, xt, "0")
        lnr = emit_ln(kb, c, "0", *stg)
        (_, _), (ost, ost_n) = emit_l0attn(
            kb, c, "0", wqkv_b, wout_b, bqk, bv, bout, alibi, cmask,
            ctx_dram, xt, yt, wcs, lnr)
        kb.wait("sync", ost, ost_n)
        kb.finalize()
    return nc, kb.input_names


def build_l1ffn_launch():
    nc = bass.Bass("TRN2", target_bir_lowering=False)
    es = contextlib.ExitStack()
    with es:
        kb = KB(nc, es)
        xt = kb.inp("xt", [D, T], F32)
        wgate = kb.inp("l1_wgate", [D, 8192], F32)
        wup = kb.inp("l1_wup", [D, 8192], F32)
        wdown = kb.inp("l1_wdown", [8192, D], F32)
        yt = kb.outp("yt", [D, T], F32)
        c = alloc_common(kb)
        prep = Prep(kb)
        wg_b = prep.cast("wgate_b", wgate, [D, 8192])
        wu_b = prep.cast("wup_b", wup, [D, 8192])
        wd_b = prep.cast("wdown_b", wdown, [8192, D])
        wcs = prep.done()
        inter = kb.scratch("inter", [8192, T], BF16)
        stg = emit_stage_x(kb, c, xt, "0")
        lnr = emit_ln(kb, c, "0", *stg)
        (_, _), (ost, ost_n) = emit_l1ffn(
            kb, c, "0", wg_b, wu_b, wd_b, inter, xt, yt, wcs, lnr)
        kb.wait("sync", ost, ost_n)
        kb.finalize()
    return nc, kb.input_names


def build_l1attn_launch():
    nc = bass.Bass("TRN2", target_bir_lowering=False)
    es = contextlib.ExitStack()
    with es:
        kb = KB(nc, es)
        xt = kb.inp("xt", [D, T], F32)
        wx = kb.inp("l1a_wx", [D, 288], F32)
        wg = kb.inp("l1a_wg", [D, D], F32)
        wo = kb.inp("l1a_wo", [D, D], F32)
        bgpp = kb.inp("l1a_bg", [128, 16], F32)
        at = kb.inp("l1a_at", [128, 32], F32)
        dppp = kb.inp("l1a_dp", [128, 16], F32)
        oh = kb.inp("l1a_oh", [32, 32 * 128], BF16)
        idm = kb.inp("l1a_id", [128, 128], BF16)
        ybs = kb.inp("l1a_yb", [1, 4 * 128], BF16)
        yt = kb.outp("yt", [D, T], F32)
        c = alloc_common(kb)
        prep = Prep(kb)
        wx_b = prep.cast("wx_b", wx, [D, 288])
        wg_b = prep.cast("wg1_b", wg, [D, D])
        wo_b = prep.cast("wo1_b", wo, [D, D])
        wcs = prep.done()
        gate_dram = kb.scratch("gate_dram", [D, T], BF16)
        y_dram = kb.scratch("y_dram", [32, T], BF16)
        yf_dram = kb.scratch("yf_dram", [D, T], BF16)
        stg = emit_stage_x(kb, c, xt, "0")
        lnr = emit_ln(kb, c, "0", *stg)
        (_, _), (ost, ost_n) = emit_l1attn(
            kb, c, "0", wx_b, wg_b, wo_b, bgpp, at, dppp, oh, idm, ybs,
            gate_dram, y_dram, yf_dram, xt, yt, wcs, lnr)
        kb.wait("sync", ost, ost_n)
        kb.finalize()
    return nc, kb.input_names


def build_fused_launch():
    nc = bass.Bass("TRN2", target_bir_lowering=False)
    es = contextlib.ExitStack()
    with es:
        kb = KB(nc, es)
        xt = kb.inp("xt", [D, T], F32)
        wqkvt = kb.inp("l0a_wqkvt", [D, 6144], F32)
        woutt = kb.inp("l0a_woutt", [D, D], F32)
        bqk = kb.inp("l0a_bqk", [128, 32], F32)
        bv = kb.inp("l0a_bv", [128, T], F32)
        bout = kb.inp("l0a_bout", [128, 16], F32)
        alibi = kb.inp("l0a_alibi", [128, 16 * 34], F32)
        cmask = kb.inp("l0a_cmask", [128, 4 * 512], BF16)
        wgate = kb.inp("l1_wgate", [D, 8192], F32)
        wup = kb.inp("l1_wup", [D, 8192], F32)
        wdown = kb.inp("l1_wdown", [8192, D], F32)
        wx = kb.inp("l1a_wx", [D, 288], F32)
        wg = kb.inp("l1a_wg", [D, D], F32)
        wo = kb.inp("l1a_wo", [D, D], F32)
        bgpp = kb.inp("l1a_bg", [128, 16], F32)
        at = kb.inp("l1a_at", [128, 32], F32)
        dppp = kb.inp("l1a_dp", [128, 16], F32)
        oh = kb.inp("l1a_oh", [32, 32 * 128], BF16)
        idm = kb.inp("l1a_id", [128, 128], BF16)
        ybs = kb.inp("l1a_yb", [1, 4 * 128], BF16)
        yt = kb.outp("yt", [D, T], F32)
        c = alloc_common(kb)

        aux = kb.sbuf("aux", [128, 4096], F32)
        auxi = aux[:, :].bitcast(I32)
        auxb = aux[:, :].bitcast(BF16)
        dec_bufs = {"spk": [auxi[:, 0:1024], auxi[:, 1024:2048]],
                    "lihi": auxi[:, 2048:4096]}
        l0a_bufs = {"bqk": aux[:, 0:32], "bv": aux[:, 32:2080],
                    "bout": aux[:, 2080:2096], "al": aux[:, 2096:2640],
                    "cm": auxb[:, 5280:7328]}
        l1a_bufs = {"at": aux[:, 0:32], "dp": aux[:, 32:48],
                    "bg": aux[:, 48:64], "yb": auxb[0:1, 128:640],
                    "db": auxb[0:32, 640:2688]}

        wg_dec, wu_dec, wd_dec, dready = emit_l0decode(kb, c, "D",
                                                       bufs=dec_bufs)
        prep = Prep(kb,
                    pf=[c["PF"][1][:, 0:1024], c["PF"][1][:, 1024:2048]],
                    po=[c["PO"][1][:, 0:1024], c["PO"][1][:, 1024:2048]])
        wqkv_b = prep.cast("wqkvt_b", wqkvt, [D, 6144])
        wout_b = prep.cast("woutt_b", woutt, [D, D])
        wg1f_b = prep.cast("wgate_b", wgate, [D, 8192])
        wu1f_b = prep.cast("wup_b", wup, [D, 8192])
        wd1f_b = prep.cast("wdown_b", wdown, [8192, D])
        wx_b = prep.cast("wx_b", wx, [D, 288])
        wg1a_b = prep.cast("wg1_b", wg, [D, D])
        wo1a_b = prep.cast("wo1_b", wo, [D, D])
        wcs = prep.done()
        for eng in ("sync", "tensor", "vector", "scalar"):
            for (s_, t_) in list(wcs) + list(dready):
                kb.wait(eng, s_, t_)

        inter0 = kb.scratch("inter0", [16384, T], BF16)
        inter1 = kb.scratch("inter1", [8192, T], BF16)
        ctx_dram = kb.scratch("ctx_dram", [D, T], BF16)
        gate_dram = kb.scratch("gate_dram", [D, T], BF16)
        y_dram = kb.scratch("y_dram", [32, T], BF16)
        yf_dram = kb.scratch("yf_dram", [D, T], BF16)
        xb = [kb.scratch("xb0", [D, T], F32), kb.scratch("xb1", [D, T], F32)]

        def barrier(ss, sn, os_, on):
            for eng in ("sync", "tensor", "vector", "scalar"):
                kb.wait(eng, os_, on)
                kb.wait(eng, ss, sn)

        stg = emit_stage_x(kb, c, xt, "SX")
        lnr = emit_ln(kb, c, "N", *stg)
        (ss, sn), (os_, on) = emit_l0ffn(
            kb, c, "A", wg_dec, wu_dec, wd_dec, inter0, xt, xb[0], dready,
            lnr)
        barrier(ss, sn, os_, on)
        lnr = emit_ln(kb, c, "N", ss, sn)
        (ss, sn), (os_, on) = emit_l0attn(
            kb, c, "B", wqkv_b, wout_b, bqk, bv, bout, alibi, cmask,
            ctx_dram, xb[0], xb[1], wcs, lnr, bufs=l0a_bufs)
        barrier(ss, sn, os_, on)
        lnr = emit_ln(kb, c, "N", ss, sn)
        (ss, sn), (os_, on) = emit_l0ffn(
            kb, c, "A", wg_dec, wu_dec, wd_dec, inter0, xb[1], xb[0], dready,
            lnr)
        barrier(ss, sn, os_, on)
        lnr = emit_ln(kb, c, "N", ss, sn)
        (ss, sn), (os_, on) = emit_l1ffn(
            kb, c, "E", wg1f_b, wu1f_b, wd1f_b, inter1, xb[0], xb[1], wcs,
            lnr)
        barrier(ss, sn, os_, on)
        lnr = emit_ln(kb, c, "N", ss, sn)
        (ss, sn), (os_, on) = emit_l1attn(
            kb, c, "F", wx_b, wg1a_b, wo1a_b, bgpp, at, dppp, oh, idm, ybs,
            gate_dram, y_dram, yf_dram, xb[1], xb[0], wcs, lnr,
            bufs=l1a_bufs)
        barrier(ss, sn, os_, on)
        lnr = emit_ln(kb, c, "N", ss, sn)
        (ss, sn), (os_, on) = emit_l1ffn(
            kb, c, "E", wg1f_b, wu1f_b, wd1f_b, inter1, xb[0], yt, wcs, lnr)
        kb.wait("sync", os_, on)
        kb.finalize()
    return nc, kb.input_names


_PROGRAMS = {}


def _get_program(name):
    if name not in _PROGRAMS:
        _PROGRAMS[name] = {
            "l0ffn": build_l0ffn_launch,
            "l0attn": build_l0attn_launch,
            "l1ffn": build_l1ffn_launch,
            "l1attn": build_l1attn_launch,
            "fused": build_fused_launch,
        }[name]()
    return _PROGRAMS[name]


def kernel_unfused(**inputs):
    inp = {k: np.asarray(v) for k, v in inputs.items()}
    x = inp["L0_attn_x"]
    NC = 8

    consts = host_consts()
    w_l0f = host_l0ffn_inputs(
        inp["L0_ffn_gate_W_packed"], inp["L0_ffn_gate_scales"],
        inp["L0_ffn_up_W_packed"], inp["L0_ffn_up_scales"],
        inp["L0_ffn_down_W_packed"], inp["L0_ffn_down_scales"])
    w_l0a = host_l0attn_inputs(
        inp["L0_attn_Wqkv"], inp["L0_attn_b_qkv"],
        inp["L0_attn_Wout"], inp["L0_attn_b_out"])
    w_l1f = {
        "l1_wgate": np.ascontiguousarray(inp["L1_ffn_w_gate"][SIGMA, :]),
        "l1_wup": np.ascontiguousarray(inp["L1_ffn_w_up"][SIGMA, :]),
        "l1_wdown": np.ascontiguousarray(inp["L1_ffn_w_down"][:, SIGMA]),
    }
    w_l1a = host_l1attn_inputs(
        inp["L1_attn_Wx"], inp["L1_attn_Wg"], inp["L1_attn_bg"],
        inp["L1_attn_Wo"], inp["L1_attn_A"], inp["L1_attn_Dp"])

    cur = [np.ascontiguousarray(x[i].T[SIGMA, :]).astype(np.float32)
           for i in range(NC)]

    seq = [("l0ffn", w_l0f), ("l0attn", w_l0a), ("l0ffn", w_l0f),
           ("l1ffn", w_l1f), ("l1attn", w_l1a), ("l1ffn", w_l1f)]
    for name, wd in seq:
        nc, names = _get_program(name)
        in_maps = []
        for i in range(NC):
            m = dict(consts)
            m.update(wd)
            m["xt"] = cur[i]
            in_maps.append({k: m[k] for k in names})
        res = run_bass_kernel_spmd(nc, in_maps, core_ids=list(range(NC)))
        cur = [np.ascontiguousarray(res.results[i]["yt"]) for i in range(NC)]

    out = np.stack([cur[i][SIGMA_INV, :].T for i in range(NC)], axis=0)
    return out.astype(np.float32)


def kernel(**inputs):
    inp = {k: np.asarray(v) for k, v in inputs.items()}
    x = inp["L0_attn_x"]
    NC = 8

    m = host_consts()
    m.update(host_l0ffn_inputs(
        inp["L0_ffn_gate_W_packed"], inp["L0_ffn_gate_scales"],
        inp["L0_ffn_up_W_packed"], inp["L0_ffn_up_scales"],
        inp["L0_ffn_down_W_packed"], inp["L0_ffn_down_scales"]))
    m.update(host_l0attn_inputs(
        inp["L0_attn_Wqkv"], inp["L0_attn_b_qkv"],
        inp["L0_attn_Wout"], inp["L0_attn_b_out"]))
    m["l1_wgate"] = np.ascontiguousarray(inp["L1_ffn_w_gate"][SIGMA, :])
    m["l1_wup"] = np.ascontiguousarray(inp["L1_ffn_w_up"][SIGMA, :])
    m["l1_wdown"] = np.ascontiguousarray(inp["L1_ffn_w_down"][:, SIGMA])
    m.update(host_l1attn_inputs(
        inp["L1_attn_Wx"], inp["L1_attn_Wg"], inp["L1_attn_bg"],
        inp["L1_attn_Wo"], inp["L1_attn_A"], inp["L1_attn_Dp"]))

    nc, names = _get_program("fused")
    in_maps = []
    for i in range(NC):
        mi = dict(m)
        mi["xt"] = np.ascontiguousarray(x[i].T[SIGMA, :]).astype(np.float32)
        in_maps.append({k: mi[k] for k in names})
    res = run_bass_kernel_spmd(nc, in_maps, core_ids=list(range(NC)))
    out = np.stack([res.results[i]["yt"][SIGMA_INV, :].T
                    for i in range(NC)], axis=0)
    return out.astype(np.float32)
